# version 20
# baseline (speedup 1.0000x reference)
import numpy as np
from contextlib import ExitStack
import concourse.bass as bass
import concourse.mybir as mybir
from concourse.bass_utils import run_bass_kernel_spmd
from concourse.replica_groups import maybe_share_collective_output_space

F32 = mybir.dt.float32
F32R = mybir.dt.float32r
BF16 = mybir.dt.bfloat16
I32 = mybir.dt.int32
AF = mybir.ActivationFunctionType
ALU = mybir.AluOpType
AX = mybir.AxisListType
IOA = bass.IndirectOffsetOnAxis

P = 128
T = 16384
TC = 2048
TH = 1024
H = 1024
DSTATE = 16
DTR = 32
E = 8
I = 2048
TB = 1024
NB = T // TB
NCORES = 8
HC = H // P
IT = I // P
RG = [list(range(NCORES))]
NT = TC // P
CAP = 512
SLOTS = E * CAP
NTS = SLOTS // P


_DEBUG_DIRS = (0, 1)
_DEBUG_DUMP = "r"
_DEBUG_TAP = None


class Ctr:
    def __init__(self, sem, step=1):
        self.sem, self.v, self.step = sem, 0, step

    def inc(self, inst, n=None):
        n = self.step if n is None else n
        inst.then_inc(self.sem, n)
        self.v += n
        return self.v


def build_program(stage=2):
    nc = bass.Bass("TRN2", target_bir_lowering=False, debug=False)

    xk_tok_h = nc.dram_tensor("xk_tok", [H, TC], F32, kind="ExternalInput").ap()
    xk_str_h = nc.dram_tensor("xk_str", [P, T], F32, kind="ExternalInput").ap()
    w1_h = nc.dram_tensor("w1", [H, P], F32, kind="ExternalInput").ap()
    dtw_h = nc.dram_tensor("dtw", [DTR, 2 * P], F32, kind="ExternalInput").ap()
    dtb_h = nc.dram_tensor("dtb", [P, 2], F32, kind="ExternalInput").ap()
    alog_h = nc.dram_tensor("alog", [P, 2 * DSTATE], F32, kind="ExternalInput").ap()
    dvec_h = nc.dram_tensor("dvec", [P, 2], F32, kind="ExternalInput").ap()
    gw_h = nc.dram_tensor("gw", [H, E], F32, kind="ExternalInput").ap()
    shg_h = nc.dram_tensor("shg", [IT, P, HC, P], F32, kind="ExternalInput").ap()
    shu_h = nc.dram_tensor("shu", [IT, P, HC, P], F32, kind="ExternalInput").ap()
    shd_h = nc.dram_tensor("shd", [HC, P, IT, P], F32, kind="ExternalInput").ap()
    rtg_h = nc.dram_tensor("rtg", [E, IT, P, HC, P], F32, kind="ExternalInput").ap()
    rtu_h = nc.dram_tensor("rtu", [E, IT, P, HC, P], F32, kind="ExternalInput").ap()
    rtd_h = nc.dram_tensor("rtd", [E, HC, P, IT, P], F32, kind="ExternalInput").ap()

    if stage >= 2:
        out_h = nc.dram_tensor("out", [TC, H], F32, kind="ExternalOutput").ap()
        dbg_h = (nc.dram_tensor("dbg", [TC, H], F32, kind="ExternalOutput").ap()
                 if _DEBUG_TAP else None)
    if stage == 1:
        rdump_h = nc.dram_tensor("rdump", [P, T], F32, kind="ExternalOutput").ap()

    dbc_loc = nc.dram_tensor("dbc_loc", [P, TC], F32, kind="Internal").ap()
    ag_space = maybe_share_collective_output_space("AllGather", RG)
    dbc_ag = nc.dram_tensor("dbc_ag", [NCORES * P, TC], F32, kind="Internal",
                            addr_space=ag_space).ap()
    r_loc = nc.dram_tensor("r_loc", [P, T], F32, kind="Internal").ap()
    rag_out = nc.dram_tensor("rag_out", [NCORES * P, T], F32, kind="Internal",
                             addr_space=ag_space).ap()
    x1_dram = nc.dram_tensor("x1_dram", [TC + 1, H], F32, kind="Internal").ap()
    y_dram = nc.dram_tensor("y_dram", [TC + 1, H], F32, kind="Internal").ap()
    sm_dram = nc.dram_tensor("sm_dram", [TC + 1, 1], F32, kind="Internal").ap()
    tok_dram = nc.dram_tensor("tok_dram", [SLOTS, 1], I32, kind="Internal").ap()

    ctx = ExitStack()
    with ctx:
        x_str = ctx.enter_context(nc.sbuf_tensor("x_str", [P, T], F32))
        y_sb = ctx.enter_context(nc.sbuf_tensor("y_sb", [P, T], F32))
        ident = ctx.enter_context(nc.sbuf_tensor("ident", [P, P], F32))
        identb = ctx.enter_context(nc.sbuf_tensor("identb", [P, P], BF16))
        gw_bf = ctx.enter_context(nc.sbuf_tensor("gw_bf", [P, HC * E], BF16))

        stackAB = ExitStack()
        chain = stackAB.enter_context(nc.sbuf_tensor("chain", [P, 2 * DSTATE], F32))
        selb16 = stackAB.enter_context(nc.sbuf_tensor("selb16", [80, DSTATE * P], F32R))
        dtwr = stackAB.enter_context(nc.sbuf_tensor("dtwr", [DTR, 2 * P], F32R))
        dtb_sb = stackAB.enter_context(nc.sbuf_tensor("dtb_sb", [P, 2], F32))
        acol = stackAB.enter_context(nc.sbuf_tensor("acol", [P, 2 * DSTATE], F32))
        dsum = stackAB.enter_context(nc.sbuf_tensor("dsum", [P, 2], F32))

        sems = {}
        for name in ["s_ld", "s_dbcd", "s_mm", "s_sc", "s_v", "s_g", "s_coll",
                     "s_st", "s_wd", "s_ffv", "s_ffs", "s_ffm",
                     "s_xc0", "s_xc1", "s_xstr", "s_rl", "s_r0", "s_r1",
                     "s_w0", "s_w1", "s_w2", "s_w3", "s_w4", "s_w5"]:
            sems[name] = ctx.enter_context(nc.semaphore(name))
        S_LD, S_DBCD, S_MM, S_SC, S_V, S_G = (sems[k] for k in
                                              ["s_ld", "s_dbcd", "s_mm", "s_sc", "s_v", "s_g"])
        S_COLL, S_ST, S_WD, S_FFV, S_FFS, S_FFM = (sems[k] for k in
                                                   ["s_coll", "s_st", "s_wd", "s_ffv", "s_ffs", "s_ffm"])
        block = ctx.enter_context(nc.Block())

        ld = Ctr(S_LD, 16)
        dbcd = Ctr(S_DBCD, 16)
        mm = Ctr(S_MM)
        sc = Ctr(S_SC)
        vv = Ctr(S_V)
        gg = Ctr(S_G)
        coll = Ctr(S_COLL)
        st = Ctr(S_ST, 16)

        stackA = ExitStack()
        xc_sb = stackA.enter_context(nc.sbuf_tensor("xc_sb", [P, 2 * TC], F32))
        xc_r = stackA.enter_context(nc.sbuf_tensor("xc_r", [P, 2 * TC], F32R))
        w1r = stackA.enter_context(nc.sbuf_tensor("w1r", [P, HC * P], F32R))
        w1f = stackA.enter_context(nc.sbuf_tensor("w1f", [P, HC * P], F32))
        dtwf = stackA.enter_context(nc.sbuf_tensor("dtwf", [DTR, 2 * P], F32))
        sel_f = stackA.enter_context(nc.sbuf_tensor("sel_f", [80, DSTATE * P], F32))
        ps_dbc = stackA.enter_context(nc.psum_tensor("ps_dbc", [P, TC], F32))

        @block.sync
        def _(sync):
            sync.dma_start(out=x_str[:], in_=xk_str_h[:]).then_inc(sems["s_xstr"], 16)
            ld.inc(sync.dma_start(out=w1f.ap().rearrange("p (c m) -> p c m", c=HC),
                                  in_=w1_h.rearrange("(c p) m -> p c m", p=P)))
            ld.inc(sync.dma_start(out=dtwf[:], in_=dtw_h[:]))
            ld.inc(sync.dma_start(out=dtb_sb[:], in_=dtb_h[:]))
            ld.inc(sync.dma_start(out=acol[:], in_=alog_h[:]))
            ld.inc(sync.dma_start(out=dsum[:], in_=dvec_h[:]))
            sync.wait_ge(S_LD, 16 * 5)
            for c in range(HC):
                if c >= 2:
                    sync.wait_ge(S_MM, c - 1)
                sync.dma_start(
                    out=xc_sb[:, (c % 2) * TC:(c % 2 + 1) * TC],
                    in_=xk_tok_h[c * P:(c + 1) * P, :]).then_inc(
                        sems["s_xc0" if c % 2 == 0 else "s_xc1"], 16)
        LD_ALL = ld.v

        @block.gpsimd
        def _(g):
            g.memset(y_sb[:, 0:P], 0.0)
            g.affine_select(out=ident[:], in_=y_sb[:, 0:P], compare_op=ALU.not_equal,
                            fill=1.0, base=0, pattern=[[-1, P]], channel_multiplier=1)
            g.memset(sel_f[:], 0.0)
            for rb in (32, 64):
                g.affine_select(out=sel_f[rb:rb + DSTATE, :], in_=sel_f[0:DSTATE, :],
                                compare_op=ALU.not_equal, fill=1.0, base=0,
                                pattern=[[1, DSTATE], [0, P]], channel_multiplier=-1)
                g.tensor_copy(out=selb16[rb:rb + DSTATE, :], in_=sel_f[rb:rb + DSTATE, :])
            g.tensor_copy(out=identb[:], in_=ident[:])
            gg.inc(g.memset(y_sb[:], 0.0))
        GG_CONST = gg.v

        @block.scalar
        def _(s):
            s.wait_ge(S_LD, 16 * 5)
            sc.inc(s.activation(out=acol[:], in_=acol[:], func=AF.Exp))
            s.wait_ge(S_SC, sc.v)
            sc.inc(s.mul(out=acol[:], in_=acol[:], mul=-1.0))
        SC_ACOL = sc.v

        @block.vector
        def _(v):
            v.wait_ge(S_LD, 16 * 5)
            vv.inc(v.tensor_tensor(out=dsum[:, 0:1], in0=dsum[:, 0:1], in1=dsum[:, 1:2], op=ALU.add))
            v.wait_ge(S_V, vv.v)
            vv.inc(v.tensor_scalar_add(out=dsum[:, 0:1], in0=dsum[:, 0:1], scalar1=1.0))
            v.wait_ge(S_V, vv.v)
            vv.inc(v.tensor_copy(out=w1r[:], in_=w1f[:]))
            vv.inc(v.tensor_copy(out=dtwr[:], in_=dtwf[:]))
        VV_SMALL = vv.v

        for c in range(HC):
            @block.vector
            def _(v, c=c):
                v.wait_ge(sems["s_xc0" if c % 2 == 0 else "s_xc1"], 16 * (c // 2 + 1))
                vv.inc(v.tensor_copy(out=xc_r[:, (c % 2) * TC:(c % 2 + 1) * TC],
                                     in_=xc_sb[:, (c % 2) * TC:(c % 2 + 1) * TC]))

            @block.tensor
            def _(t, c=c, vt=vv.v):
                t.wait_ge(S_V, vt)
                last = None
                for nk in range(TC // 512):
                    last = t.matmul(
                        out=ps_dbc[:, nk * 512:(nk + 1) * 512],
                        lhsT=w1r[:, c * P:(c + 1) * P],
                        rhs=xc_r[:, (c % 2) * TC + nk * 512:(c % 2) * TC + (nk + 1) * 512],
                        start=(c == 0), stop=(c == HC - 1))
                mm.inc(last)
        MM_DBC = mm.v

        @block.vector
        def _(v):
            v.wait_ge(S_MM, MM_DBC)
            vv.inc(v.tensor_copy(out=xc_sb[:, 0:TC], in_=ps_dbc[:]))
        VV_DBCE = vv.v

        @block.sync
        def _(sync):
            sync.wait_ge(S_V, VV_DBCE)
            st.inc(sync.dma_start(out=dbc_loc[:], in_=xc_sb[:, 0:TC]))
        ST_DBC = st.v

        @block.gpsimd
        def _(g):
            g.wait_ge(S_ST, ST_DBC)
            coll.inc(g.collective_compute("AllGather", ALU.bypass, replica_groups=RG,
                                          ins=[dbc_loc], outs=[dbc_ag]))
        COLL_AG = coll.v

        stackA.close()

        stackB = ExitStack()
        dbc_sb = stackB.enter_context(nc.sbuf_tensor("dbc_sb", [96, TB], F32))
        dbc_r = stackB.enter_context(nc.sbuf_tensor("dbc_r", [96, 2 * TB], F32R))
        spu = stackB.enter_context(nc.sbuf_tensor("spu", [P, TB], F32))
        delta = stackB.enter_context(nc.sbuf_tensor("delta", [P, TB], F32))
        a_sb = stackB.enter_context(nc.sbuf_tensor("a_sb", [P, 2 * TB], F32))
        bx_sb = stackB.enter_context(nc.sbuf_tensor("bx_sb", [P, TB], F32))
        h_sb = stackB.enter_context(nc.sbuf_tensor("h_sb", [P, 2 * TB], F32))
        hc_sb = stackB.enter_context(nc.sbuf_tensor("hc_sb", [P, 2 * TB], F32))
        pool = [stackB.enter_context(nc.psum_tensor(f"pp{i}", [P, TB], F32))
                for i in range(4)]
        pool_guard = [(None, 0)] * 4
        pool_i = [0]

        def next_slot():
            s_ = pool_i[0] % 4
            pool_i[0] += 1
            return s_

        dbc_ag_v = dbc_ag.rearrange("(r p) t -> r p t", p=P)
        blk_order = []
        if 0 in _DEBUG_DIRS:
            blk_order += [(0, bi) for bi in range(NB)]
        if 1 in _DEBUG_DIRS:
            blk_order += [(1, bi) for bi in reversed(range(NB))]

        v_round = {}
        v_bx = {}
        v_hc = {}
        v_scan = {}
        v_yadd = {}
        g_iter = {}
        sc_iter = {}
        mm_end = {}
        sc_sp_prev = None
        v_u_prev = None
        NS = DSTATE

        def gs_of(idx, s):
            return idx * NS + s

        def by_gs(gs):
            return (gs // NS, gs % NS)

        for idx, (dr, bi) in enumerate(blk_order):
            pb = idx % 2
            rowbase = 64 * dr
            sender = bi // 2
            col0 = (bi % 2) * TB

            @block.sync
            def _(sync, pb=pb, rowbase=rowbase, sender=sender, col0=col0, idx=idx):
                if idx == 0:
                    sync.wait_ge(S_COLL, COLL_AG)
                if idx >= 1:
                    sync.wait_ge(S_V, v_round[idx - 1])
                dbcd.inc(sync.dma_start(
                    out=dbc_sb[0:DTR, :],
                    in_=dbc_ag_v[sender, rowbase:rowbase + DTR, col0:col0 + TB]))
                dbcd.inc(sync.dma_start(
                    out=dbc_sb[32:32 + DSTATE, :],
                    in_=dbc_ag_v[sender, rowbase + 32:rowbase + 48, col0:col0 + TB]))
                dbcd.inc(sync.dma_start(
                    out=dbc_sb[64:64 + DSTATE, :],
                    in_=dbc_ag_v[sender, rowbase + 48:rowbase + 64, col0:col0 + TB]))
            DBCD_T = dbcd.v

            @block.vector
            def _(v, pb=pb, idx=idx):
                v.wait_ge(S_DBCD, DBCD_T)
                if idx >= 2:
                    v.wait_ge(S_MM, mm_end[idx - 2])
                o = pb * TB
                v.tensor_copy(out=dbc_r[0:DTR, o:o + TB], in_=dbc_sb[0:DTR, :])
                v.tensor_copy(out=dbc_r[32:48, o:o + TB], in_=dbc_sb[32:48, :])
                vv.inc(v.tensor_copy(out=dbc_r[64:80, o:o + TB], in_=dbc_sb[64:80, :]))
            v_round[idx] = vv.v

            dslot = next_slot()
            @block.tensor
            def _(t, dr=dr, pb=pb, vt=vv.v, idx=idx, dslot=dslot, gd=pool_guard[dslot]):
                t.wait_ge(S_V, vt)
                if idx == 0:
                    t.wait_ge(S_SC, SC_ACOL)
                if gd[1]:
                    t.wait_ge(gd[0], gd[1])
                last = None
                for nk in range(TB // 512):
                    last = t.matmul(out=pool[dslot][:, nk * 512:(nk + 1) * 512],
                                    lhsT=dtwr[:, dr * P:(dr + 1) * P],
                                    rhs=dbc_r[0:DTR, pb * TB + nk * 512:pb * TB + (nk + 1) * 512],
                                    start=True, stop=True)
                mm.inc(last)
            MM_D = mm.v

            @block.scalar
            def _(s, dr=dr, idx=idx, dslot=dslot):
                s.wait_ge(S_MM, MM_D)
                if idx >= 1:
                    s.wait_ge(S_V, v_bx[gs_of(idx - 1, NS - 1)])
                s.activation(out=spu[:], in_=pool[dslot][:], func=AF.Exp,
                             bias=dtb_sb[:, dr:dr + 1])
                sc.inc(s.activation(out=delta[:], in_=spu[:], func=AF.Ln, bias=1.0))
            sc_sp_prev = sc.v
            pool_guard[dslot] = (S_SC, sc.v)

            @block.vector
            def _(v, bi=bi, idx=idx, sct=sc.v):
                v.wait_ge(S_SC, sct)
                if idx == 0:
                    v.wait_ge(sems["s_xstr"], 16)
                vv.inc(v.tensor_tensor(out=spu[:], in0=delta[:],
                                       in1=x_str[:, bi * TB:(bi + 1) * TB], op=ALU.mult))
            v_u_prev = vv.v

            for s in range(NS):
                sb = s % 2
                gs = gs_of(idx, s)
                sb2 = gs % 2

                bslot = next_slot()
                cslot = next_slot()
                @block.tensor
                def _(t, s=s, pb=pb, bslot=bslot, cslot=cslot,
                      gb=pool_guard[bslot], gc=pool_guard[cslot], vt=v_round[idx]):
                    t.wait_ge(S_V, vt)
                    if gb[1]:
                        t.wait_ge(gb[0], gb[1])
                    if gc[1]:
                        t.wait_ge(gc[0], gc[1])
                    last = None
                    for nk in range(TB // 512):
                        t.matmul(out=pool[bslot][:, nk * 512:(nk + 1) * 512],
                                 lhsT=selb16[32:48, s * P:(s + 1) * P],
                                 rhs=dbc_r[32:48, pb * TB + nk * 512:pb * TB + (nk + 1) * 512],
                                 start=True, stop=True)
                        last = t.matmul(out=pool[cslot][:, nk * 512:(nk + 1) * 512],
                                        lhsT=selb16[64:80, s * P:(s + 1) * P],
                                        rhs=dbc_r[64:80, pb * TB + nk * 512:pb * TB + (nk + 1) * 512],
                                        start=True, stop=True)
                    mm.inc(last)
                MM_BC = mm.v

                @block.scalar
                def _(s_, s=s, sb=sb, dr=dr, gs=gs):
                    s_.wait_ge(S_MM, MM_BC)
                    if gs >= 2:
                        s_.wait_ge(S_V, v_scan[gs - 2])
                    sc.inc(s_.activation(out=a_sb[:, sb * TB:(sb + 1) * TB],
                                         in_=delta[:], func=AF.Exp,
                                         scale=acol[:, dr * DSTATE + s:dr * DSTATE + s + 1]))
                sc_iter[gs] = sc.v

                @block.vector
                def _(v, s=s, sb=sb, sb2=sb2, dr=dr, bi=bi, idx=idx, gs=gs,
                      bslot=bslot, cslot=cslot, mt=MM_BC, sct=sc.v):
                    v.wait_ge(S_MM, mt)
                    v.wait_ge(S_SC, sct)
                    vv.inc(v.tensor_tensor(out=bx_sb[:], in0=spu[:],
                                           in1=pool[bslot][:], op=ALU.mult))
                    v_bx[gs] = vv.v
                    if dr == 1:
                        v.wait_ge(S_V, vv.v)
                    if gs >= 2:
                        v.wait_ge(S_G, g_iter[gs - 2])
                    first = (idx == 0) or (dr == 1 and blk_order[idx - 1][0] == 0)
                    ci = dr * DSTATE + s
                    init = 0.0 if first else chain[:, ci:ci + 1]
                    ho = sb2 * TB
                    if dr == 0:
                        ii = v.tensor_tensor_scan(out=h_sb[:, ho:ho + TB],
                                                  data0=a_sb[:, sb * TB:(sb + 1) * TB],
                                                  data1=bx_sb[:], initial=init,
                                                  op0=ALU.mult, op1=ALU.add)
                    else:
                        ii = v.tensor_tensor_scan(out=h_sb[:, ho:ho + TB],
                                                  data0=a_sb[:, sb * TB:(sb + 1) * TB][:, ::-1],
                                                  data1=bx_sb[:, ::-1], initial=init,
                                                  op0=ALU.mult, op1=ALU.add)
                    vv.inc(ii)
                    v_scan[gs] = vv.v
                    if dr == 1:
                        v.wait_ge(S_V, vv.v)
                    if gs >= 2 and (gs - 2) in v_hc:
                        v.wait_ge(S_G, g_iter[gs - 2])
                    if dr == 0:
                        ii = v.tensor_tensor(out=hc_sb[:, ho:ho + TB],
                                             in0=h_sb[:, ho:ho + TB],
                                             in1=pool[cslot][:], op=ALU.mult)
                    else:
                        ii = v.tensor_tensor(out=hc_sb[:, ho:ho + TB],
                                             in0=h_sb[:, ho:ho + TB][:, ::-1],
                                             in1=pool[cslot][:], op=ALU.mult)
                    vv.inc(ii)
                    v_hc[gs] = vv.v
                pool_guard[bslot] = (S_V, v_bx[gs])
                pool_guard[cslot] = (S_V, v_hc[gs])

                @block.gpsimd
                def _(g, sb2=sb2, dr=dr, bi=bi, s=s, gs=gs, vt=v_hc[gs]):
                    g.wait_ge(S_V, vt)
                    ho = sb2 * TB
                    ci = dr * DSTATE + s
                    if dr == 0:
                        g.tensor_copy(out=chain[:, ci:ci + 1],
                                      in_=h_sb[:, ho + TB - 1:ho + TB])
                        src = hc_sb[:, ho:ho + TB]
                    else:
                        g.tensor_copy(out=chain[:, ci:ci + 1],
                                      in_=h_sb[:, ho + TB - 1:ho + TB])
                        src = hc_sb[:, ho:ho + TB]
                    gg.inc(g.tensor_tensor(out=y_sb[:, bi * TB:(bi + 1) * TB],
                                           in0=y_sb[:, bi * TB:(bi + 1) * TB],
                                           in1=src, op=ALU.add))
                g_iter[gs] = gg.v

            mm_end[idx] = mm.v

        r_st = []
        for j in range(NCORES):
            rbuf = a_sb if j % 2 == 0 else h_sb

            @block.vector
            def _(v, j=j, rbuf=rbuf):
                if j == 0:
                    lastgs = gs_of(len(blk_order) - 1, NS - 1)
                    v.wait_ge(S_G, g_iter[lastgs])
                if j >= 2:
                    v.wait_ge(sems["s_r0" if j % 2 == 0 else "s_r1"], 16 * (j // 2))
                ii = None
                for q in range(2):
                    blk = slice(j * TC + q * TB, j * TC + (q + 1) * TB)
                    ii = v.scalar_tensor_tensor(out=rbuf[:, q * TB:(q + 1) * TB],
                                                in0=x_str[:, blk], scalar=dsum[:, 0:1],
                                                in1=y_sb[:, blk], op0=ALU.mult, op1=ALU.add)
                vv.inc(ii)

            @block.sync
            def _(sync, j=j, rbuf=rbuf, vt=vv.v):
                sync.wait_ge(S_V, vt)
                if stage == 1:
                    src = rbuf[:] if _DEBUG_DUMP == "r" else y_sb[:, j * TC:(j + 1) * TC]
                    st.inc(sync.dma_start(out=rdump_h[:, j * TC:(j + 1) * TC], in_=src))
                sync.dma_start(out=r_loc[:, j * TC:(j + 1) * TC],
                               in_=rbuf[:]).then_inc(
                                   sems["s_r0" if j % 2 == 0 else "s_r1"], 16)
            r_st.append(st.v)

        @block.gpsimd
        def _(g, stv=st.v):
            g.wait_ge(S_ST, stv)
            g.wait_ge(sems["s_r0"], 16 * 4)
            g.wait_ge(sems["s_r1"], 16 * 4)
            coll.inc(g.collective_compute("AllGather", ALU.bypass, replica_groups=RG,
                                          ins=[r_loc], outs=[rag_out]))
        COLL_A2A = coll.v

        stackB.close()
        stackAB.close()

        if stage == 1:
            @block.sync
            def _(sync):
                sync.wait_ge(S_COLL, COLL_A2A)
                sync.wait_ge(S_ST, st.v)
            return nc

        stackD = ExitStack()
        NGRAN = 4
        wld = stackD.enter_context(nc.sbuf_tensor("wld", [P, NGRAN * 1024], F32))
        wbf = stackD.enter_context(nc.sbuf_tensor("wbf", [P, NGRAN * 1024], BF16))
        sact = stackD.enter_context(nc.sbuf_tensor("sact", [P, TH], F32))
        lgsm = stackD.enter_context(nc.sbuf_tensor("lgsm", [E, TH], F32))
        mskf = stackD.enter_context(nc.sbuf_tensor("mskf", [E, TH], F32))
        cnt = stackD.enter_context(nc.sbuf_tensor("cnt", [E, TH], F32))
        stat = stackD.enter_context(nc.sbuf_tensor("stat", [P, 24], F32))
        lgtm = stackD.enter_context(nc.sbuf_tensor("lgtm", [P, NT * E], F32))
        smtm = stackD.enter_context(nc.sbuf_tensor("smtm", [P, NT * E], F32))
        cnttm = stackD.enter_context(nc.sbuf_tensor("cnttm", [P, NT * E], F32))
        rsc = stackD.enter_context(nc.sbuf_tensor("rsc", [P, NT * E], F32))
        e4c = stackD.enter_context(nc.sbuf_tensor("e4c", [P, NT * E], F32))
        iota_t = stackD.enter_context(nc.sbuf_tensor("iota_t", [P, NT], I32))
        pref_t = stackD.enter_context(nc.sbuf_tensor("pref_t", [P, NTS], I32))
        idx_sct = stackD.enter_context(nc.sbuf_tensor("idx_sct", [P, NT], I32))
        idx_g = stackD.enter_context(nc.sbuf_tensor("idx_g", [P, NTS], I32))
        sm_slot = stackD.enter_context(nc.sbuf_tensor("sm_slot", [P, NTS], F32))
        smv = stackD.enter_context(nc.sbuf_tensor("smv", [16, P], F32))
        act_cmp = stackD.enter_context(nc.sbuf_tensor("act_cmp", [P, IT * CAP], BF16))
        yc = stackD.enter_context(nc.sbuf_tensor("yc", [P, 8 * 512], F32))
        ps_gu = [stackD.enter_context(nc.psum_tensor(f"ps_gu{i}", [P, 512], F32))
                 for i in range(4)]
        ps_y = stackD.enter_context(nc.psum_tensor("ps_y", [P, 1024], F32))
        ps_ty = stackD.enter_context(nc.psum_tensor("ps_ty", [P, 1024], F32))

        r_fm = x_str[:, 0:8 * TC]
        x1_tm = y_sb[:, 0:NT * H]
        x1_bf = x_str[:, 0:4 * TC].bitcast(BF16)
        act_sh = x_str[:, 4 * TC:8 * TC].bitcast(BF16)
        xs_fm = x_str[:, 4 * TC:6 * TC].bitcast(BF16)
        xg = x_str[:, 6 * TC:6 * TC + 1024]
        xs_tok = x_str[:, 6 * TC + 1024:6 * TC + 2048].bitcast(BF16)
        y_tok = x_str[:, 7 * TC:8 * TC]
        r2 = x_str[:, 0:NT * H]

        fm = Ctr(S_FFM)
        fv = Ctr(S_FFV)
        fs = Ctr(S_FFS)
        wd = Ctr(S_WD, 16)
        gx = Ctr(sems["s_xc0"], 16)
        gx.v = 64
        gsm = Ctr(sems["s_xc1"], 16)
        gsm.v = 64
        ysc = Ctr(sems["s_r0"], 16)
        ysc.v = 64
        yl0 = Ctr(sems["s_r1"], 16)
        yl0.v = 64
        yl1 = Ctr(S_LD, 16)
        yl1.v = LD_ALL

        def pst_of(j):
            return ps_ty[:, (j % 2) * 512:(j % 2) * 512 + P]

        @block.sync
        def _(sync):
            sync.wait_ge(S_COLL, COLL_A2A)
            wd.inc(sync.dma_start(out=wld[:, 0:HC * E].rearrange("p (c m) -> p c m", c=HC),
                                  in_=gw_h.rearrange("(c p) m -> p c m", p=P)))

        @block.vector
        def _(v, wt=wd.v):
            v.wait_ge(S_WD, wt)
            fv.inc(v.tensor_copy(out=gw_bf[:], in_=wld[:, 0:HC * E]))
        FV_GW = fv.v

        @block.gpsimd
        def _(g):
            g.wait_ge(S_COLL, COLL_A2A)
            g.iota(out=iota_t[:], pattern=[[P, NT]], base=0, channel_multiplier=1)
            g.iota(out=pref_t[:], pattern=[[0, NTS]], base=TC, channel_multiplier=0)
            g.iota(out=e4c[:], pattern=[[0, NT], [4, E]], base=0, channel_multiplier=0,
                   allow_small_or_imprecise_dtypes=True)
            g.memset(cnt[:], 0.0)
            dbcd.inc(g.dma_start(out=tok_dram.rearrange("(p f) o -> p f o", p=P),
                                 in_=pref_t[:].rearrange("p (f o) -> p f o", o=1)))
            dbcd.inc(g.dma_start(out=x1_dram[TC:TC + 1, :], in_=cnt[0:1, :]))
            dbcd.inc(g.dma_start(out=sm_dram[TC:TC + 1, :], in_=cnt[0:1, 0:1]))
        TOK_PREF = dbcd.v

        rag_v = rag_out.rearrange("(c p) t -> p c t", p=P)

        wsems = [sems[f"s_w{i}"] for i in range(NGRAN)]
        wld_free = [0] * NGRAN
        wld_free_sem = [None] * NGRAN
        wld_free_sem[0] = S_FFV
        wld_free[0] = FV_GW
        wld_uses = [0] * NGRAN
        wbf_free = [0] * NGRAN
        ring = {"w": 0, "b": 0}

        def emit_wload(src_ap, ngran):
            while True:
                cand = ring["w"] % NGRAN
                if ngran == 2 and cand % 2 == 1:
                    ring["w"] += 1
                    continue
                g0 = cand
                ring["w"] += ngran
                break
            waits = [(wld_free_sem[g0 + k], wld_free[g0 + k]) for k in range(ngran)]
            n_tr = 4 * ngran

            @block.sync
            def _(sync, g0=g0, src_ap=src_ap, ngran=ngran, waits=waits, n_tr=n_tr):
                for sem_, wt in waits:
                    if wt:
                        sync.wait_ge(sem_, wt)
                dst = wld[:, g0 * 1024:(g0 + ngran) * 1024].rearrange(
                    "p (c m) -> p c m", m=P)
                for j in range(n_tr):
                    sync.dma_start(out=dst[:, 2 * j:2 * j + 2],
                                   in_=src_ap[:, 2 * j:2 * j + 2]).then_inc(
                                       wsems[g0], 16)
            wld_uses[g0] += 16 * n_tr
            mark = (wsems[g0], wld_uses[g0])
            return g0, mark

        WC_ROT = ["v", "g"]
        wc_i = [0]

        def emit_wcast(g0, wt, ngran):
            eng = WC_ROT[wc_i[0] % len(WC_ROT)]
            wc_i[0] += 1
            while True:
                cand = ring["b"] % NGRAN
                if ngran == 2 and cand % 2 == 1:
                    ring["b"] += 1
                    continue
                b0 = cand
                ring["b"] += ngran
                break
            waits = [wbf_free[b0 + k] for k in range(ngran)]

            def body(e_):
                e_.wait_ge(wt[0], wt[1])
                for w_ in waits:
                    if w_:
                        e_.wait_ge(S_FFM, w_)

            if eng == "v":
                @block.vector
                def _(v):
                    body(v)
                    fv.inc(v.tensor_copy(out=wbf[:, b0 * 1024:(b0 + ngran) * 1024],
                                         in_=wld[:, g0 * 1024:(g0 + ngran) * 1024]))
                mark_sem, mark = S_FFV, fv.v
            elif eng == "s":
                @block.scalar
                def _(s):
                    body(s)
                    fs.inc(s.copy(out=wbf[:, b0 * 1024:(b0 + ngran) * 1024],
                                  in_=wld[:, g0 * 1024:(g0 + ngran) * 1024]))
                mark_sem, mark = S_FFS, fs.v
            else:
                @block.gpsimd
                def _(g):
                    body(g)
                    gg.inc(g.tensor_copy(out=wbf[:, b0 * 1024:(b0 + ngran) * 1024],
                                         in_=wld[:, g0 * 1024:(g0 + ngran) * 1024]))
                mark_sem, mark = S_G, gg.v
            for k in range(ngran):
                wld_free_sem[g0 + k] = mark_sem
                wld_free[g0 + k] = mark
            return b0, (mark_sem, mark)

        @block.sync
        def _(sync):
            sync.wait_ge(S_COLL, COLL_A2A)
            pid = nc.partition_id([mybir.EngineType.SP])
            for c in range(HC):
                sync.dma_start(out=r_fm[:, c * TC:(c + 1) * TC],
                               in_=rag_v[:, c, bass.ts(pid, TC)]).then_inc(
                                   sems["s_rl"], 16)

        s_tp = {}
        n_tp = 0
        for c in range(HC):
            for tt in range(NT):
                @block.tensor
                def _(t, c=c, tt=tt, n_tp=n_tp):
                    if n_tp == 0:
                        t.wait_ge(sems["s_rl"], 16 * HC)
                    if n_tp >= 2:
                        t.wait_ge(S_FFV, s_tp[n_tp - 2])
                    fm.inc(t.transpose(out=pst_of(n_tp),
                                       in_=r_fm[:, c * TC + tt * P:c * TC + (tt + 1) * P],
                                       identity=ident[:]))

                @block.vector
                def _(v, c=c, tt=tt, n_tp=n_tp, mt=fm.v):
                    v.wait_ge(S_FFM, mt)
                    fv.inc(v.tensor_copy(out=x1_tm[:, tt * H + c * P:tt * H + (c + 1) * P],
                                         in_=pst_of(n_tp)))
                s_tp[n_tp] = fv.v
                n_tp += 1
        FS_RTM = fv.v

        def emit_ln(buf, tt, first_wait):
            @block.vector
            def _(v, tt=tt, fw=first_wait):
                if fw is not None:
                    v.wait_ge(fw[0], fw[1])
                fv.inc(v.tensor_tensor(out=sact[:], in0=buf[:, tt * H:(tt + 1) * H],
                                       in1=buf[:, tt * H:(tt + 1) * H], op=ALU.mult))
                v.wait_ge(S_FFV, fv.v)
                v.reduce_sum(out=stat[:, 0:1], in_=buf[:, tt * H:(tt + 1) * H], axis=AX.X)
                fv.inc(v.reduce_sum(out=stat[:, 1:2], in_=sact[:], axis=AX.X))
                v.wait_ge(S_FFV, fv.v)
                v.tensor_scalar_mul(out=stat[:, 0:1], in0=stat[:, 0:1], scalar1=1.0 / H)
                fv.inc(v.tensor_scalar_mul(out=stat[:, 1:2], in0=stat[:, 1:2], scalar1=1.0 / H))
                v.wait_ge(S_FFV, fv.v)
                fv.inc(v.tensor_tensor(out=stat[:, 2:3], in0=stat[:, 0:1],
                                       in1=stat[:, 0:1], op=ALU.mult))
                v.wait_ge(S_FFV, fv.v)
                fv.inc(v.tensor_tensor(out=stat[:, 1:2], in0=stat[:, 1:2],
                                       in1=stat[:, 2:3], op=ALU.subtract))
                v.wait_ge(S_FFV, fv.v)
                fv.inc(v.tensor_scalar_add(out=stat[:, 1:2], in0=stat[:, 1:2],
                                           scalar1=1e-5))
                v.wait_ge(S_FFV, fv.v)
                fv.inc(v.reciprocal(out=stat[:, 2:3], in_=stat[:, 1:2]))

            @block.scalar
            def _(s, vt=fv.v):
                s.wait_ge(S_FFV, vt)
                fs.inc(s.sqrt(out=stat[:, 3:4], in_=stat[:, 2:3]))

            @block.vector
            def _(v, tt=tt, st_=fs.v):
                v.wait_ge(S_FFS, st_)
                fv.inc(v.tensor_scalar(out=buf[:, tt * H:(tt + 1) * H],
                                       in0=buf[:, tt * H:(tt + 1) * H],
                                       scalar1=stat[:, 0:1], scalar2=stat[:, 3:4],
                                       op0=ALU.subtract, op1=ALU.mult))
            return fv.v

        for tt in range(NT):
            emit_ln(x1_tm, tt, (S_FFV, FS_RTM) if tt == 0 else None)
        FV_LN1 = fv.v

        @block.sync
        def _(sync, vt=FV_LN1):
            sync.wait_ge(S_FFV, vt)
            for qq in range(4):
                dbcd.inc(sync.dma_start(
                    out=x1_dram[qq * 512:(qq + 1) * 512, :].rearrange(
                        "(tt p) f -> p tt f", p=P),
                    in_=x1_tm[:, qq * 4 * H:(qq + 1) * 4 * H].rearrange(
                        "p (tt f) -> p tt f", tt=4)))
        TOK_X1 = dbcd.v

        s_tp2 = {}
        n_tp2 = 0
        for c in range(HC):
            for tt in range(NT):
                @block.tensor
                def _(t, c=c, tt=tt, n_tp2=n_tp2):
                    if n_tp2 == 0:
                        t.wait_ge(S_FFV, FV_LN1)
                    if n_tp2 >= 2:
                        t.wait_ge(S_FFV, s_tp2[n_tp2 - 2])
                    fm.inc(t.transpose(out=pst_of(n_tp2),
                                       in_=x1_tm[:, tt * H + c * P:tt * H + (c + 1) * P],
                                       identity=ident[:]))

                @block.vector
                def _(v, c=c, tt=tt, n_tp2=n_tp2, mt=fm.v):
                    v.wait_ge(S_FFM, mt)
                    fv.inc(v.tensor_copy(out=x1_bf[:, c * TC + tt * P:c * TC + (tt + 1) * P],
                                         in_=pst_of(n_tp2)))
                s_tp2[n_tp2] = fv.v
                n_tp2 += 1
        FS_X1BF = fv.v

        v_lgt = {}
        FS_LG = 0
        for lp in range(2):
            @block.tensor
            def _(t, lp=lp, prev=FS_LG):
                t.wait_ge(S_FFV, FS_X1BF)
                t.wait_ge(S_FFV, FV_GW)
                if lp == 1:
                    t.wait_ge(S_FFV, prev)
                last = None
                for c in range(HC):
                    for nk in range(2):
                        last = t.matmul(
                            out=ps_y[0:E, nk * 512:(nk + 1) * 512],
                            lhsT=gw_bf[:, c * E:(c + 1) * E],
                            rhs=x1_bf[:, c * TC + lp * TH + nk * 512:
                                      c * TC + lp * TH + (nk + 1) * 512],
                            start=(c == 0), stop=(c == HC - 1))
                fm.inc(last)

            @block.vector
            def _(v, lp=lp, mt=fm.v):
                v.wait_ge(S_FFM, mt)
                fv.inc(v.tensor_copy(out=lgsm[:], in_=ps_y[0:E, :]))
            FS_LG = fv.v

            for ttl in range(8):
                tt = lp * 8 + ttl

                @block.tensor
                def _(t, ttl=ttl, tt=tt, lg=FS_LG):
                    if ttl == 0:
                        t.wait_ge(S_FFV, lg)
                    if tt >= 2:
                        t.wait_ge(S_FFV, v_lgt[tt - 2])
                    fm.inc(t.transpose(out=pst_of(tt)[0:P, 0:E],
                                       in_=lgsm[:, ttl * P:(ttl + 1) * P],
                                       identity=ident[0:E, 0:E]))

                @block.vector
                def _(v, tt=tt, mt=fm.v):
                    v.wait_ge(S_FFM, mt)
                    fv.inc(v.tensor_copy(out=lgtm[:, tt * E:(tt + 1) * E],
                                         in_=pst_of(tt)[0:P, 0:E]))
                v_lgt[tt] = fv.v
        psy_guard = [(S_FFV, FS_LG), (S_FFV, FS_LG)]

        for tt in range(NT):
            @block.vector
            def _(v, tt=tt, vt=v_lgt[NT - 1]):
                if tt == 0:
                    v.wait_ge(S_FFV, vt)
                fv.inc(v.reduce_max(out=stat[:, 4 + tt:5 + tt],
                                    in_=lgtm[:, tt * E:(tt + 1) * E], axis=AX.X))
                v.wait_ge(S_FFV, fv.v)
                fv.inc(v.tensor_single_scalar(out=smtm[:, tt * E:(tt + 1) * E],
                                              in_=lgtm[:, tt * E:(tt + 1) * E],
                                              scalar=stat[:, 4 + tt:5 + tt],
                                              op=ALU.is_equal))
        FV_EQ = fv.v

        @block.scalar
        def _(s, vt=FV_EQ):
            s.wait_ge(S_FFV, vt)
            fs.inc(s.activation(out=stat[:, 4:20], in_=stat[:, 4:20], func=AF.Sigmoid))
        FS_SIG = fs.v

        tp_hist = {-1: (S_FFV, FV_EQ)}
        tp_i = [1]
        v_scan = [0]
        fm_cntt = [0]

        @block.tensor
        def _(t, sg=FS_SIG, lg=v_lgt[NT - 2]):
            t.wait_ge(S_FFS, sg)
            t.wait_ge(S_FFV, lg)
            fm.inc(t.transpose(out=pst_of(0)[0:16, 0:P], in_=stat[:, 4:20],
                               identity=ident[:]))

        @block.vector
        def _(v, mt=fm.v):
            v.wait_ge(S_FFM, mt)
            fv.inc(v.tensor_copy(out=smv[:], in_=pst_of(0)[0:16, 0:P]))
        tp_hist[0] = (S_FFV, fv.v)

        @block.sync
        def _(sync, vt=fv.v):
            sync.wait_ge(S_FFV, vt)
            dbcd.inc(sync.dma_start(
                out=sm_dram[0:TC, :].rearrange("(a b) o -> a (b o)", a=16),
                in_=smv[:]))
        TOK_SM = dbcd.v
        for q in range(2):
            for ttl in range(8):
                tt = q * 8 + ttl
                j = tp_i[0]
                tp_i[0] += 1

                @block.tensor
                def _(t, ttl=ttl, tt=tt, j=j):
                    hw = tp_hist.get(j - 2)
                    if hw is not None:
                        t.wait_ge(hw[0], hw[1])
                    fm.inc(t.transpose(out=pst_of(j)[0:E, 0:P],
                                       in_=smtm[:, tt * E:(tt + 1) * E],
                                       identity=ident[:]))

                @block.vector
                def _(v, ttl=ttl, j=j, q=q, mt=fm.v, vs=v_scan[0]):
                    v.wait_ge(S_FFM, mt)
                    if q == 1 and ttl == 0:
                        v.wait_ge(S_FFV, vs)
                    fv.inc(v.tensor_copy(out=mskf[:, ttl * P:(ttl + 1) * P],
                                         in_=pst_of(j)[0:E, 0:P]))
                tp_hist[j] = (S_FFV, fv.v)
            FS_MSK = fv.v

            @block.vector
            def _(v, q=q, st_=FS_MSK, mt=fm_cntt[0]):
                v.wait_ge(S_FFV, st_)
                v.wait_ge(S_DBCD, TOK_PREF)
                if q == 1:
                    v.wait_ge(S_FFM, mt)
                    fv.inc(v.tensor_copy(out=stat[0:E, 23:24], in_=cnt[:, TH - 1:TH]))
                    v.wait_ge(S_FFV, fv.v)
                fv.inc(v.tensor_tensor_scan(out=cnt[:], data0=mskf[:], data1=mskf[:],
                                            initial=0.0 if q == 0 else stat[0:E, 23:24],
                                            op0=ALU.add, op1=ALU.add))
            v_scan[0] = fv.v

            for ttl in range(8):
                tt = q * 8 + ttl
                j = tp_i[0]
                tp_i[0] += 1

                @block.tensor
                def _(t, ttl=ttl, j=j, vt=v_scan[0]):
                    if ttl == 0:
                        t.wait_ge(S_FFV, vt)
                    hw = tp_hist[j - 2]
                    t.wait_ge(hw[0], hw[1])
                    fm.inc(t.transpose(out=pst_of(j)[0:P, 0:E],
                                       in_=cnt[:, ttl * P:(ttl + 1) * P],
                                       identity=ident[0:E, 0:E]))

                @block.vector
                def _(v, tt=tt, j=j, mt=fm.v):
                    v.wait_ge(S_FFM, mt)
                    fv.inc(v.tensor_copy(out=cnttm[:, tt * E:(tt + 1) * E],
                                         in_=pst_of(j)[0:P, 0:E]))
                tp_hist[j] = (S_FFV, fv.v)
            fm_cntt[0] = fm.v
        FV_CNTT = fv.v

        @block.vector
        def _(v):
            v.wait_ge(S_FFV, FV_CNTT)
            fv.inc(v.tensor_scalar(out=rsc[:], in0=cnttm[:], scalar1=0.5,
                                   scalar2=-1.0, op0=ALU.mult, op1=ALU.add))
            v.wait_ge(S_FFV, fv.v)
            fv.inc(v.tensor_scalar(out=rsc[:], in0=rsc[:], scalar1=511.0,
                                   scalar2=None, op0=ALU.min))
            v.wait_ge(S_FFV, fv.v)
            fv.inc(v.tensor_scalar(out=cnttm[:], in0=rsc[:], scalar1=128.0,
                                   scalar2=None, op0=ALU.is_ge))
            fv.inc(v.tensor_scalar(out=lgtm[:], in0=rsc[:], scalar1=256.0,
                                   scalar2=None, op0=ALU.is_ge))
            v.wait_ge(S_FFV, fv.v)
            fv.inc(v.tensor_tensor(out=cnttm[:], in0=cnttm[:], in1=lgtm[:], op=ALU.add))
            fv.inc(v.tensor_scalar(out=lgtm[:], in0=rsc[:], scalar1=384.0,
                                   scalar2=None, op0=ALU.is_ge))
            v.wait_ge(S_FFV, fv.v)
            fv.inc(v.tensor_tensor(out=cnttm[:], in0=cnttm[:], in1=lgtm[:], op=ALU.add))
            fv.inc(v.tensor_scalar(out=rsc[:], in0=rsc[:], scalar1=32.0,
                                   scalar2=None, op0=ALU.mult))
            v.wait_ge(S_FFV, fv.v)
            fv.inc(v.scalar_tensor_tensor(out=rsc[:], in0=cnttm[:], scalar=-4095.0,
                                          in1=rsc[:], op0=ALU.mult, op1=ALU.add))
            v.wait_ge(S_FFV, fv.v)
            fv.inc(v.tensor_tensor(out=rsc[:], in0=rsc[:], in1=e4c[:], op=ALU.add))
            v.wait_ge(S_FFV, fv.v)
            fv.inc(v.tensor_tensor(out=rsc[:], in0=rsc[:], in1=smtm[:], op=ALU.mult))
            v.wait_ge(S_FFV, fv.v)
            for tt in range(NT):
                fv.inc(v.reduce_sum(out=smtm[:, tt:tt + 1],
                                    in_=rsc[:, tt * E:(tt + 1) * E], axis=AX.X))
            v.wait_ge(S_FFV, fv.v)
            fv.inc(v.tensor_copy(out=idx_sct[:], in_=smtm[:, 0:NT]))
        FV_IDX = fv.v

        slot_guard_s = [0, 0]
        slot_guard_v = [0, 0]
        ty_hist = {-2: (S_FFV, FV_CNTT), -1: (S_FFV, FV_CNTT)}
        ty_i = [0]

        def ty_half():
            j = ty_i[0]
            ty_i[0] += 1
            return j, ps_ty[:, (j % 2) * 512:(j % 2 + 1) * 512], ty_hist[j - 2]
        yc_free = (None, 0)
        FM_SHFREE = 0
        prem_prev = 0
        xs_ready = {}
        xs_done_d = {}
        ysc_k = [0]
        TOK_IDX = 0
        GSM_ALL = gsm.v

        def emit_prep(e):
            nonlocal prem_prev
            sl = e % 2
            for tl in range(4):
                tau = 4 * e + tl

                @block.gpsimd
                def _(g, tau=tau, e=e, tl=tl, pp=prem_prev, ti=TOK_IDX,
                      shf=FM_SHFREE):
                    if e == 0 and tl == 0:
                        g.wait_ge(S_DBCD, ti)
                        g.wait_ge(S_FFM, shf)
                    if pp:
                        g.wait_ge(S_FFV, pp)
                    gx.inc(g.indirect_dma_start(
                        out=xg[:], out_offset=None,
                        in_=x1_dram[:],
                        in_offset=IOA(ap=idx_g[:, tau:tau + 1], axis=0)))

                @block.vector
                def _(v, tau=tau, gxv=gx.v, gall=GSM_ALL):
                    v.wait_ge(sems["s_xc0"], gxv)
                    v.wait_ge(sems["s_xc1"], gall)
                    fv.inc(v.tensor_scalar(
                        out=xs_tok[:, (tau % 2) * 1024:(tau % 2 + 1) * 1024],
                        in0=xg[:], scalar1=sm_slot[:, tau:tau + 1],
                        scalar2=None, op0=ALU.mult))
                prem_prev = fv.v

                for hg in range(2):
                    tyj, tyh, tyw = ty_half()

                    @block.tensor
                    def _(t, tau=tau, hg=hg, vt=fv.v, tyh=tyh, tyw=tyw):
                        if hg == 0:
                            t.wait_ge(S_FFV, vt)
                        t.wait_ge(tyw[0], tyw[1])
                        last = None
                        tyb = tyh.bitcast(BF16)
                        for cl in range(4):
                            c = hg * 4 + cl
                            last = t.transpose(
                                out=tyb[:, cl * P:(cl + 1) * P],
                                in_=xs_tok[:, (tau % 2) * 1024 + c * P:
                                           (tau % 2) * 1024 + (c + 1) * P],
                                identity=identb[:])
                        fm.inc(last)

                    @block.vector
                    def _(v, tau=tau, tl=tl, sl=sl, hg=hg, mt=fm.v, tyh=tyh,
                          xr=xs_ready.get(e - 2, 0)):
                        v.wait_ge(S_FFM, mt)
                        if tl == 0 and hg == 0 and xr:
                            v.wait_ge(S_FFM, xr)
                        fv.inc(v.tensor_copy(
                            out=xs_fm[:, sl * 4096:(sl + 1) * 4096].rearrange(
                                "p (c s) -> p c s", s=CAP)[
                                :, hg * 4:(hg + 1) * 4, tl * P:(tl + 1) * P],
                            in_=tyh.bitcast(BF16)[:, 0:4 * P].rearrange(
                                "p (c s) -> p c s", s=P)))
                    ty_hist[tyj] = (S_FFV, fv.v)
                xs_done_d[e] = fv.v

        for ei in range(9):
            if ei == 0:
                for vh in range(2):
                    for it in range(IT):
                        g0, wt0 = emit_wload(shg_h[it], 1)
                        bg, (bgs, bgv) = emit_wcast(g0, wt0, 1)
                        g1, wt1 = emit_wload(shu_h[it], 1)
                        bu, (bus, buv) = emit_wcast(g1, wt1, 1)

                        @block.tensor
                        def _(t, it=it, vh=vh, bg=bg, bu=bu, bgs=bgs, bgv=bgv,
                              bus=bus, buv=buv,
                              sg=list(slot_guard_s), sv=list(slot_guard_v)):
                            t.wait_ge(bgs, bgv)
                            t.wait_ge(bus, buv)
                            if it == 0 and vh == 0:
                                t.wait_ge(S_FFV, FS_X1BF)
                            for nk in range(2):
                                if sg[nk]:
                                    t.wait_ge(S_FFS, sg[nk])
                                if sv[nk]:
                                    t.wait_ge(S_FFV, sv[nk])
                            last = None
                            for c in range(HC):
                                for nk in range(2):
                                    last = t.matmul(
                                        out=ps_gu[nk][:],
                                        lhsT=wbf[:, bg * 1024 + c * P:bg * 1024 + (c + 1) * P],
                                        rhs=x1_bf[:, c * TC + vh * TH + nk * 512:
                                                  c * TC + vh * TH + (nk + 1) * 512],
                                        start=(c == 0), stop=(c == HC - 1))
                            fm.inc(last)
                            for c in range(HC):
                                for nk in range(2):
                                    last = t.matmul(
                                        out=ps_gu[2 + nk][:],
                                        lhsT=wbf[:, bu * 1024 + c * P:bu * 1024 + (c + 1) * P],
                                        rhs=x1_bf[:, c * TC + vh * TH + nk * 512:
                                                  c * TC + vh * TH + (nk + 1) * 512],
                                        start=(c == 0), stop=(c == HC - 1))
                            fm.inc(last)
                        FM_G = fm.v - 1
                        FM_GU = fm.v
                        wbf_free[bg] = fm.v
                        wbf_free[bu] = fm.v
                        for nk in range(2):
                            @block.scalar
                            def _(s, nk=nk, mt=FM_G):
                                s.wait_ge(S_FFM, mt)
                                fs.inc(s.activation(out=sact[:, nk * 512:(nk + 1) * 512],
                                                    in_=ps_gu[nk][:], func=AF.Silu))
                            slot_guard_s[nk] = fs.v

                            @block.vector
                            def _(v, it=it, nk=nk, st_=fs.v, mt=FM_GU):
                                v.wait_ge(S_FFS, st_)
                                v.wait_ge(S_FFM, mt)
                                fv.inc(v.tensor_tensor(
                                    out=act_sh[:, it * TH + nk * 512:it * TH + (nk + 1) * 512],
                                    in0=sact[:, nk * 512:(nk + 1) * 512],
                                    in1=ps_gu[2 + nk][:], op=ALU.mult))
                            slot_guard_v[nk] = fv.v
                    FV_ACTSH = fv.v

                    for ht in range(HC):
                        gd, wtd = emit_wload(shd_h[ht], 2)
                        bd, (cts, ctv) = emit_wcast(gd, wtd, 2)

                        @block.tensor
                        def _(t, ht=ht, bd=bd, cts=cts, ctv=ctv, at=FV_ACTSH,
                              pg=list(psy_guard)):
                            t.wait_ge(cts, ctv)
                            t.wait_ge(S_FFV, at)
                            for hh in range(2):
                                if pg[hh][1]:
                                    t.wait_ge(pg[hh][0], pg[hh][1])
                            last = None
                            for ic in range(IT):
                                for nk in range(2):
                                    last = t.matmul(
                                        out=ps_y[:, nk * 512:(nk + 1) * 512],
                                        lhsT=wbf[:, bd * 1024 + ic * P:bd * 1024 + (ic + 1) * P],
                                        rhs=act_sh[:, ic * TH + nk * 512:ic * TH + (nk + 1) * 512],
                                        start=(ic == 0), stop=(ic == IT - 1))
                            fm.inc(last)
                        wbf_free[bd] = fm.v
                        wbf_free[bd + 1] = fm.v
                        FM_DN = fm.v

                        @block.vector
                        def _(v, ht=ht, mt=FM_DN, ycf=yc_free):
                            v.wait_ge(S_FFM, mt)
                            if ycf[1]:
                                v.wait_ge(ycf[0], ycf[1])
                            fv.inc(v.tensor_copy(out=yc[:, (ht % 2) * 1024:(ht % 2 + 1) * 1024],
                                                 in_=ps_y[:]))
                        FS_YSH = fv.v
                        psy_guard[0] = (S_FFV, fv.v)
                        psy_guard[1] = (S_FFV, fv.v)

                        for hg in range(2):
                            tyj, tyh, tyw = ty_half()

                            @block.tensor
                            def _(t, ht=ht, hg=hg, st_=FS_YSH, tyh=tyh, tyw=tyw):
                                if hg == 0:
                                    t.wait_ge(S_FFV, st_)
                                t.wait_ge(tyw[0], tyw[1])
                                last = None
                                for tl_ in range(4):
                                    ttl = hg * 4 + tl_
                                    last = t.transpose(
                                        out=tyh[:, tl_ * P:(tl_ + 1) * P],
                                        in_=yc[:, (ht % 2) * 1024 + ttl * P:
                                               (ht % 2) * 1024 + (ttl + 1) * P],
                                        identity=ident[:])
                                fm.inc(last)

                            @block.vector
                            def _(v, ht=ht, vh=vh, hg=hg, mt=fm.v, x1st=TOK_X1,
                                  tyh=tyh):
                                v.wait_ge(S_FFM, mt)
                                if ht == 0 and vh == 0 and hg == 0:
                                    v.wait_ge(S_DBCD, x1st)
                                base = vh * 8 + hg * 4
                                fv.inc(v.tensor_tensor(
                                    out=x1_tm.rearrange("p (tt f) -> p tt f", f=H)[
                                        :, base:base + 4, ht * P:(ht + 1) * P],
                                    in0=x1_tm.rearrange("p (tt f) -> p tt f", f=H)[
                                        :, base:base + 4, ht * P:(ht + 1) * P],
                                    in1=tyh.rearrange("p (tt f) -> p tt f", f=P),
                                    op=ALU.add))
                            ty_hist[tyj] = (S_FFV, fv.v)
                        yc_free = (S_FFV, fv.v)
                FM_SHFREE = fm.v
            else:
                e = ei - 1
                sl = e % 2
                if ei == 1:
                    @block.gpsimd
                    def _(g):
                        g.wait_ge(S_FFV, FV_IDX)
                        g.wait_ge(S_DBCD, TOK_PREF)
                        for tt in range(NT):
                            dbcd.inc(g.indirect_dma_start(
                                out=tok_dram[:],
                                out_offset=IOA(ap=idx_sct[:, tt:tt + 1], axis=0),
                                in_=iota_t[:, tt:tt + 1], in_offset=None))
                    TOK_SCT = dbcd.v

                    @block.gpsimd
                    def _(g):
                        g.wait_ge(S_DBCD, TOK_SCT)
                        dbcd.inc(g.dma_start(
                            out=idx_g[:].rearrange("p (f o) -> p f o", o=1),
                            in_=tok_dram.rearrange("(p f) o -> p f o", p=P)))
                    TOK_IDX = dbcd.v

                    @block.gpsimd
                    def _(g):
                        g.wait_ge(S_DBCD, TOK_IDX)
                        for j in range(NTS):
                            gsm.inc(g.indirect_dma_start(
                                out=sm_slot[:, j:j + 1], out_offset=None,
                                in_=sm_dram[:],
                                in_offset=IOA(ap=idx_g[:, j:j + 1], axis=0)))
                    GSM_ALL = gsm.v

                if e == 0:
                    emit_prep(0)
                xs_done = xs_done_d[e]

                for it in range(IT):
                    g0, wt0 = emit_wload(rtg_h[e][it], 1)
                    bg, (bgs, bgv) = emit_wcast(g0, wt0, 1)
                    g1, wt1 = emit_wload(rtu_h[e][it], 1)
                    bu, (bus, buv) = emit_wcast(g1, wt1, 1)

                    @block.tensor
                    def _(t, it=it, sl=sl, bg=bg, bu=bu, bgs=bgs, bgv=bgv,
                          bus=bus, buv=buv, xsd=xs_done,
                          sg=slot_guard_s[it % 2], sv=slot_guard_v[it % 2]):
                        t.wait_ge(bgs, bgv)
                        t.wait_ge(bus, buv)
                        t.wait_ge(S_FFV, xsd)
                        if sg:
                            t.wait_ge(S_FFS, sg)
                        if sv:
                            t.wait_ge(S_FFV, sv)
                        last = None
                        for c in range(HC):
                            last = t.matmul(
                                out=ps_gu[it % 2][:],
                                lhsT=wbf[:, bg * 1024 + c * P:bg * 1024 + (c + 1) * P],
                                rhs=xs_fm[:, sl * 4096 + c * CAP:sl * 4096 + (c + 1) * CAP],
                                start=(c == 0), stop=(c == HC - 1))
                        fm.inc(last)
                        for c in range(HC):
                            last = t.matmul(
                                out=ps_gu[2 + it % 2][:],
                                lhsT=wbf[:, bu * 1024 + c * P:bu * 1024 + (c + 1) * P],
                                rhs=xs_fm[:, sl * 4096 + c * CAP:sl * 4096 + (c + 1) * CAP],
                                start=(c == 0), stop=(c == HC - 1))
                        fm.inc(last)
                    FM_G = fm.v - 1
                    FM_GU = fm.v
                    wbf_free[bg] = fm.v
                    wbf_free[bu] = fm.v

                    @block.scalar
                    def _(s, it=it, mt=FM_G):
                        s.wait_ge(S_FFM, mt)
                        fs.inc(s.activation(out=sact[:, (it % 2) * 512:(it % 2 + 1) * 512],
                                            in_=ps_gu[it % 2][:], func=AF.Silu))
                    slot_guard_s[it % 2] = fs.v

                    @block.vector
                    def _(v, it=it, st_=fs.v, mt=FM_GU):
                        v.wait_ge(S_FFS, st_)
                        v.wait_ge(S_FFM, mt)
                        fv.inc(v.tensor_tensor(
                            out=act_cmp[:, it * CAP:(it + 1) * CAP],
                            in0=sact[:, (it % 2) * 512:(it % 2 + 1) * 512],
                            in1=ps_gu[2 + it % 2][:], op=ALU.mult))
                    slot_guard_v[it % 2] = fv.v
                FV_ACT = fv.v
                xs_ready[e] = fm.v
                if e < E - 1:
                    emit_prep(e + 1)

                for ht in range(HC):
                    gd, wtd = emit_wload(rtd_h[e][ht], 2)
                    bd, (cts, ctv) = emit_wcast(gd, wtd, 2)

                    @block.tensor
                    def _(t, ht=ht, bd=bd, cts=cts, ctv=ctv, at=FV_ACT,
                          pg=psy_guard[ht % 2]):
                        t.wait_ge(cts, ctv)
                        t.wait_ge(S_FFV, at)
                        if pg[1]:
                            t.wait_ge(pg[0], pg[1])
                        last = None
                        for ic in range(IT):
                            last = t.matmul(
                                out=ps_y[:, (ht % 2) * 512:(ht % 2 + 1) * 512],
                                lhsT=wbf[:, bd * 1024 + ic * P:bd * 1024 + (ic + 1) * P],
                                rhs=act_cmp[:, ic * CAP:(ic + 1) * CAP],
                                start=(ic == 0), stop=(ic == IT - 1))
                        fm.inc(last)
                    wbf_free[bd] = fm.v
                    wbf_free[bd + 1] = fm.v
                    FM_DN = fm.v

                    @block.vector
                    def _(v, ht=ht, mt=FM_DN, ycf=yc_free):
                        v.wait_ge(S_FFM, mt)
                        if ht == 0 and ycf[1]:
                            v.wait_ge(ycf[0], ycf[1])
                        fv.inc(v.tensor_copy(out=yc[:, ht * 512:(ht + 1) * 512],
                                             in_=ps_y[:, (ht % 2) * 512:(ht % 2 + 1) * 512]))
                    psy_guard[ht % 2] = (S_FFV, fv.v)
                FS_DN = fv.v

                for tl in range(4):
                    tau = 4 * e + tl
                    k = ysc_k[0]
                    ysc_k[0] += 1

                    for hg in range(2):
                        tyj, tyh, tyw = ty_half()

                        @block.tensor
                        def _(t, tl=tl, hg=hg, st_=FS_DN, tyh=tyh, tyw=tyw):
                            if tl == 0 and hg == 0:
                                t.wait_ge(S_FFV, st_)
                            t.wait_ge(tyw[0], tyw[1])
                            last = None
                            for hl in range(4):
                                ht = hg * 4 + hl
                                last = t.transpose(
                                    out=tyh[:, hl * P:(hl + 1) * P],
                                    in_=yc[:, ht * 512 + tl * P:ht * 512 + (tl + 1) * P],
                                    identity=ident[:])
                            fm.inc(last)

                        @block.vector
                        def _(v, k=k, hg=hg, mt=fm.v, tyh=tyh):
                            v.wait_ge(S_FFM, mt)
                            if k >= 2 and hg == 0:
                                v.wait_ge(sems["s_r0"], 64 + 16 * (k - 1))
                            fv.inc(v.tensor_copy(
                                out=y_tok[:, (k % 2) * 1024 + hg * 512:
                                          (k % 2) * 1024 + (hg + 1) * 512],
                                in_=tyh))
                        ty_hist[tyj] = (S_FFV, fv.v)
                    FS_YTOK = fv.v

                    @block.gpsimd
                    def _(g, tau=tau, k=k, st_=FS_YTOK):
                        g.wait_ge(S_FFV, st_)
                        ysc.inc(g.indirect_dma_start(
                            out=y_dram[:],
                            out_offset=IOA(ap=idx_g[:, tau:tau + 1], axis=0),
                            in_=y_tok[:, (k % 2) * 1024:(k % 2 + 1) * 1024],
                            in_offset=None))
                yc_free = (S_FFM, fm.v)

        YSC_ALL = ysc.v

        r2_fv = {}
        for tt in range(NT):
            ldc = yl0 if tt % 2 == 0 else yl1

            @block.sync
            def _(sync, tt=tt, ldc=ldc):
                if tt == 0:
                    sync.wait_ge(sems["s_r0"], YSC_ALL)
                if tt >= 2:
                    sync.wait_ge(S_FFV, r2_fv[tt - 2])
                ldc.inc(sync.dma_start(
                    out=yc[:, (tt % 2) * 1024:(tt % 2 + 1) * 1024],
                    in_=y_dram[tt * P:(tt + 1) * P, :]))
            LD_T = ldc.v

            @block.vector
            def _(v, tt=tt, ldt=LD_T):
                v.wait_ge(sems["s_r1"] if tt % 2 == 0 else S_LD, ldt)
                fv.inc(v.tensor_tensor(
                    out=r2[:, tt * H:(tt + 1) * H],
                    in0=x1_tm[:, tt * H:(tt + 1) * H],
                    in1=yc[:, (tt % 2) * 1024:(tt % 2 + 1) * 1024],
                    op=ALU.add))
            r2_fv[tt] = fv.v

            emit_ln(r2, tt, None)

            @block.sync
            def _(sync, tt=tt, vt=fv.v):
                sync.wait_ge(S_FFV, vt)
                st.inc(sync.dma_start(
                    out=out_h[tt * P:(tt + 1) * P, :],
                    in_=r2[:, tt * H:(tt + 1) * H]))

        stackD.close()

        @block.sync
        def _(sync):
            sync.wait_ge(S_ST, st.v)

    return nc


def _prep_inputs(ins):
    x = np.ascontiguousarray(np.asarray(ins["L0_attn_x"])[0])
    xT = np.ascontiguousarray(x.T)
    w1 = np.ascontiguousarray(
        np.concatenate([np.asarray(ins["L0_attn_x_proj_w"]).T,
                        np.asarray(ins["L0_attn_x_proj_b_w"]).T], axis=1))
    dtw_f = np.asarray(ins["L0_attn_dt_proj_w"])
    dtw_b = np.asarray(ins["L0_attn_dt_proj_b_w"])
    alog_f = np.asarray(ins["L0_attn_A_log"])
    alog_b = np.asarray(ins["L0_attn_A_b_log"])
    dtb_f = np.asarray(ins["L0_attn_dt_proj_b"])
    dtb_b = np.asarray(ins["L0_attn_dt_proj_b_b"])
    dv_f = np.asarray(ins["L0_attn_D"])
    dv_b = np.asarray(ins["L0_attn_D_b"])

    gw = np.ascontiguousarray(np.asarray(ins["L0_ffn_gate_w"]))
    shg = np.ascontiguousarray(np.asarray(ins["L0_ffn_sh_gate"])
                               .reshape(HC, P, IT, P).transpose(2, 1, 0, 3))
    shu = np.ascontiguousarray(np.asarray(ins["L0_ffn_sh_up"])
                               .reshape(HC, P, IT, P).transpose(2, 1, 0, 3))
    shd = np.ascontiguousarray(np.asarray(ins["L0_ffn_sh_down"])
                               .reshape(IT, P, HC, P).transpose(2, 1, 0, 3))
    rtg = np.ascontiguousarray(np.asarray(ins["L0_ffn_rt_gate"])
                               .reshape(E, HC, P, IT, P).transpose(0, 3, 2, 1, 4))
    rtu = np.ascontiguousarray(np.asarray(ins["L0_ffn_rt_up"])
                               .reshape(E, HC, P, IT, P).transpose(0, 3, 2, 1, 4))
    rtd = np.ascontiguousarray(np.asarray(ins["L0_ffn_rt_down"])
                               .reshape(E, IT, P, HC, P).transpose(0, 3, 2, 1, 4))

    in_maps = []
    for k in range(NCORES):
        sl = slice(P * k, P * (k + 1))
        tok = slice(TC * k, TC * (k + 1))
        in_maps.append({
            "xk_tok": np.ascontiguousarray(xT[:, tok]),
            "xk_str": np.ascontiguousarray(xT[sl, :]),
            "w1": w1,
            "dtw": np.ascontiguousarray(
                np.concatenate([dtw_f[sl].T, dtw_b[sl].T], axis=1)),
            "dtb": np.ascontiguousarray(np.stack([dtb_f[sl], dtb_b[sl]], axis=1)),
            "alog": np.ascontiguousarray(np.concatenate([alog_f[sl], alog_b[sl]], axis=1)),
            "dvec": np.ascontiguousarray(np.stack([dv_f[sl], dv_b[sl]], axis=1)),
            "gw": gw, "shg": shg, "shu": shu, "shd": shd,
            "rtg": rtg, "rtu": rtu, "rtd": rtd,
        })
    return in_maps


def kernel(**inputs):
    in_maps = _prep_inputs(inputs)
    nc = build_program(stage=2)
    res = run_bass_kernel_spmd(nc, in_maps, core_ids=list(range(NCORES)))
    out = np.concatenate([res.results[k]["out"] for k in range(NCORES)], axis=0)
    return out.reshape(1, T, H)


# version 21 (final)
# speedup vs baseline: 1.0057x; 1.0057x over previous
import numpy as np
from contextlib import ExitStack
import concourse.bass as bass
import concourse.mybir as mybir
from concourse.bass_utils import run_bass_kernel_spmd
from concourse.replica_groups import maybe_share_collective_output_space

F32 = mybir.dt.float32
F32R = mybir.dt.float32r
BF16 = mybir.dt.bfloat16
I32 = mybir.dt.int32
AF = mybir.ActivationFunctionType
ALU = mybir.AluOpType
AX = mybir.AxisListType
IOA = bass.IndirectOffsetOnAxis

P = 128
T = 16384
TC = 2048
TH = 1024
H = 1024
DSTATE = 16
DTR = 32
E = 8
I = 2048
TB = 1024
NB = T // TB
NCORES = 8
HC = H // P
IT = I // P
RG = [list(range(NCORES))]
NT = TC // P
CAP = 512
SLOTS = E * CAP
NTS = SLOTS // P


_DEBUG_DIRS = (0, 1)
_DEBUG_DUMP = "r"
_DEBUG_TAP = None


class Ctr:
    def __init__(self, sem, step=1):
        self.sem, self.v, self.step = sem, 0, step

    def inc(self, inst, n=None):
        n = self.step if n is None else n
        inst.then_inc(self.sem, n)
        self.v += n
        return self.v


def build_program(stage=2):
    nc = bass.Bass("TRN2", target_bir_lowering=False, debug=False)

    xk_tok_h = nc.dram_tensor("xk_tok", [H, TC], F32, kind="ExternalInput").ap()
    xk_str_h = nc.dram_tensor("xk_str", [P, T], F32, kind="ExternalInput").ap()
    w1_h = nc.dram_tensor("w1", [H, P], F32, kind="ExternalInput").ap()
    dtw_h = nc.dram_tensor("dtw", [DTR, 2 * P], F32, kind="ExternalInput").ap()
    dtb_h = nc.dram_tensor("dtb", [P, 2], F32, kind="ExternalInput").ap()
    alog_h = nc.dram_tensor("alog", [P, 2 * DSTATE], F32, kind="ExternalInput").ap()
    dvec_h = nc.dram_tensor("dvec", [P, 2], F32, kind="ExternalInput").ap()
    gw_h = nc.dram_tensor("gw", [H, E], F32, kind="ExternalInput").ap()
    shg_h = nc.dram_tensor("shg", [IT, P, HC, P], F32, kind="ExternalInput").ap()
    shu_h = nc.dram_tensor("shu", [IT, P, HC, P], F32, kind="ExternalInput").ap()
    shd_h = nc.dram_tensor("shd", [HC, P, IT, P], F32, kind="ExternalInput").ap()
    rtg_h = nc.dram_tensor("rtg", [E, IT, P, HC, P], F32, kind="ExternalInput").ap()
    rtu_h = nc.dram_tensor("rtu", [E, IT, P, HC, P], F32, kind="ExternalInput").ap()
    rtd_h = nc.dram_tensor("rtd", [E, HC, P, IT, P], F32, kind="ExternalInput").ap()

    if stage >= 2:
        out_h = nc.dram_tensor("out", [TC, H], F32, kind="ExternalOutput").ap()
        dbg_h = (nc.dram_tensor("dbg", [TC, H], F32, kind="ExternalOutput").ap()
                 if _DEBUG_TAP else None)
    if stage == 1:
        rdump_h = nc.dram_tensor("rdump", [P, T], F32, kind="ExternalOutput").ap()

    dbc_loc = nc.dram_tensor("dbc_loc", [P, TC], F32, kind="Internal").ap()
    ag_space = maybe_share_collective_output_space("AllGather", RG)
    dbc_ag = nc.dram_tensor("dbc_ag", [NCORES * P, TC], F32, kind="Internal",
                            addr_space=ag_space).ap()
    r_loc = nc.dram_tensor("r_loc", [P, T], F32, kind="Internal").ap()
    rag_out = nc.dram_tensor("rag_out", [NCORES * P, T], F32, kind="Internal",
                             addr_space=ag_space).ap()
    x1_dram = nc.dram_tensor("x1_dram", [TC + 1, H], F32, kind="Internal").ap()
    y_dram = nc.dram_tensor("y_dram", [TC + 1, H], F32, kind="Internal").ap()
    sm_dram = nc.dram_tensor("sm_dram", [TC + 1, 1], F32, kind="Internal").ap()
    tok_dram = nc.dram_tensor("tok_dram", [SLOTS, 1], I32, kind="Internal").ap()

    ctx = ExitStack()
    with ctx:
        x_str = ctx.enter_context(nc.sbuf_tensor("x_str", [P, T], F32))
        y_sb = ctx.enter_context(nc.sbuf_tensor("y_sb", [P, T], F32))
        ident = ctx.enter_context(nc.sbuf_tensor("ident", [P, P], F32))
        identb = ctx.enter_context(nc.sbuf_tensor("identb", [P, P], BF16))
        gw_bf = ctx.enter_context(nc.sbuf_tensor("gw_bf", [P, HC * E], BF16))

        stackAB = ExitStack()
        chain = stackAB.enter_context(nc.sbuf_tensor("chain", [P, 2 * DSTATE], F32))
        selb16 = stackAB.enter_context(nc.sbuf_tensor("selb16", [80, DSTATE * P], F32R))
        dtwr = stackAB.enter_context(nc.sbuf_tensor("dtwr", [DTR, 2 * P], F32R))
        dtb_sb = stackAB.enter_context(nc.sbuf_tensor("dtb_sb", [P, 2], F32))
        acol = stackAB.enter_context(nc.sbuf_tensor("acol", [P, 2 * DSTATE], F32))
        dsum = stackAB.enter_context(nc.sbuf_tensor("dsum", [P, 2], F32))

        sems = {}
        for name in ["s_ld", "s_dbcd", "s_mm", "s_sc", "s_v", "s_g", "s_coll",
                     "s_st", "s_wd", "s_ffv", "s_ffs", "s_ffm",
                     "s_xc0", "s_xc1", "s_xstr", "s_rl", "s_r0", "s_r1",
                     "s_w0", "s_w1", "s_w2", "s_w3", "s_w4", "s_w5"]:
            sems[name] = ctx.enter_context(nc.semaphore(name))
        S_LD, S_DBCD, S_MM, S_SC, S_V, S_G = (sems[k] for k in
                                              ["s_ld", "s_dbcd", "s_mm", "s_sc", "s_v", "s_g"])
        S_COLL, S_ST, S_WD, S_FFV, S_FFS, S_FFM = (sems[k] for k in
                                                   ["s_coll", "s_st", "s_wd", "s_ffv", "s_ffs", "s_ffm"])
        block = ctx.enter_context(nc.Block())

        ld = Ctr(S_LD, 16)
        dbcd = Ctr(S_DBCD, 16)
        mm = Ctr(S_MM)
        sc = Ctr(S_SC)
        vv = Ctr(S_V)
        gg = Ctr(S_G)
        coll = Ctr(S_COLL)
        st = Ctr(S_ST, 16)

        stackA = ExitStack()
        xc_sb = stackA.enter_context(nc.sbuf_tensor("xc_sb", [P, 2 * TC], F32))
        xc_r = stackA.enter_context(nc.sbuf_tensor("xc_r", [P, 2 * TC], F32R))
        w1r = stackA.enter_context(nc.sbuf_tensor("w1r", [P, HC * P], F32R))
        w1f = stackA.enter_context(nc.sbuf_tensor("w1f", [P, HC * P], F32))
        dtwf = stackA.enter_context(nc.sbuf_tensor("dtwf", [DTR, 2 * P], F32))
        sel_f = stackA.enter_context(nc.sbuf_tensor("sel_f", [80, DSTATE * P], F32))
        ps_dbc = stackA.enter_context(nc.psum_tensor("ps_dbc", [P, TC], F32))

        @block.sync
        def _(sync):
            sync.dma_start(out=x_str[:], in_=xk_str_h[:]).then_inc(sems["s_xstr"], 16)
            ld.inc(sync.dma_start(out=w1f.ap().rearrange("p (c m) -> p c m", c=HC),
                                  in_=w1_h.rearrange("(c p) m -> p c m", p=P)))
            ld.inc(sync.dma_start(out=dtwf[:], in_=dtw_h[:]))
            ld.inc(sync.dma_start(out=dtb_sb[:], in_=dtb_h[:]))
            ld.inc(sync.dma_start(out=acol[:], in_=alog_h[:]))
            ld.inc(sync.dma_start(out=dsum[:], in_=dvec_h[:]))
            sync.wait_ge(S_LD, 16 * 5)
            for c in range(HC):
                if c >= 2:
                    sync.wait_ge(S_MM, c - 1)
                sync.dma_start(
                    out=xc_sb[:, (c % 2) * TC:(c % 2 + 1) * TC],
                    in_=xk_tok_h[c * P:(c + 1) * P, :]).then_inc(
                        sems["s_xc0" if c % 2 == 0 else "s_xc1"], 16)
        LD_ALL = ld.v

        @block.gpsimd
        def _(g):
            gg.inc(g.memset(y_sb[:, 0:P], 0.0))
            g.wait_ge(S_G, gg.v)
            gg.inc(g.affine_select(out=ident[:], in_=y_sb[:, 0:P],
                                   compare_op=ALU.not_equal, fill=1.0, base=0,
                                   pattern=[[-1, P]], channel_multiplier=1))
            gg.inc(g.memset(sel_f[:], 0.0))
            g.wait_ge(S_G, gg.v)
            for rb in (32, 64):
                gg.inc(g.affine_select(out=sel_f[rb:rb + DSTATE, :],
                                       in_=sel_f[0:DSTATE, :],
                                       compare_op=ALU.not_equal, fill=1.0, base=0,
                                       pattern=[[1, DSTATE], [0, P]],
                                       channel_multiplier=-1))
                g.wait_ge(S_G, gg.v)
                gg.inc(g.tensor_copy(out=selb16[rb:rb + DSTATE, :],
                                     in_=sel_f[rb:rb + DSTATE, :]))
            g.wait_ge(S_G, gg.v)
            g.tensor_copy(out=identb[:], in_=ident[:])
            gg.inc(g.memset(y_sb[:], 0.0))
        GG_CONST = gg.v

        @block.scalar
        def _(s):
            s.wait_ge(S_LD, 16 * 5)
            sc.inc(s.activation(out=acol[:], in_=acol[:], func=AF.Exp))
            s.wait_ge(S_SC, sc.v)
            sc.inc(s.mul(out=acol[:], in_=acol[:], mul=-1.0))
        SC_ACOL = sc.v

        @block.vector
        def _(v):
            v.wait_ge(S_LD, 16 * 5)
            vv.inc(v.tensor_tensor(out=dsum[:, 0:1], in0=dsum[:, 0:1], in1=dsum[:, 1:2], op=ALU.add))
            v.wait_ge(S_V, vv.v)
            vv.inc(v.tensor_scalar_add(out=dsum[:, 0:1], in0=dsum[:, 0:1], scalar1=1.0))
            v.wait_ge(S_V, vv.v)
            vv.inc(v.tensor_copy(out=w1r[:], in_=w1f[:]))
            vv.inc(v.tensor_copy(out=dtwr[:], in_=dtwf[:]))
        VV_SMALL = vv.v

        for c in range(HC):
            @block.vector
            def _(v, c=c):
                v.wait_ge(sems["s_xc0" if c % 2 == 0 else "s_xc1"], 16 * (c // 2 + 1))
                vv.inc(v.tensor_copy(out=xc_r[:, (c % 2) * TC:(c % 2 + 1) * TC],
                                     in_=xc_sb[:, (c % 2) * TC:(c % 2 + 1) * TC]))

            @block.tensor
            def _(t, c=c, vt=vv.v):
                t.wait_ge(S_V, vt)
                last = None
                for nk in range(TC // 512):
                    last = t.matmul(
                        out=ps_dbc[:, nk * 512:(nk + 1) * 512],
                        lhsT=w1r[:, c * P:(c + 1) * P],
                        rhs=xc_r[:, (c % 2) * TC + nk * 512:(c % 2) * TC + (nk + 1) * 512],
                        start=(c == 0), stop=(c == HC - 1))
                mm.inc(last)
        MM_DBC = mm.v

        @block.vector
        def _(v):
            v.wait_ge(S_MM, MM_DBC)
            vv.inc(v.tensor_copy(out=xc_sb[:, 0:TC], in_=ps_dbc[:]))
        VV_DBCE = vv.v

        @block.sync
        def _(sync):
            sync.wait_ge(S_V, VV_DBCE)
            st.inc(sync.dma_start(out=dbc_loc[:], in_=xc_sb[:, 0:TC]))
        ST_DBC = st.v

        @block.gpsimd
        def _(g):
            g.wait_ge(S_ST, ST_DBC)
            coll.inc(g.collective_compute("AllGather", ALU.bypass, replica_groups=RG,
                                          ins=[dbc_loc], outs=[dbc_ag]))
        COLL_AG = coll.v

        stackA.close()

        stackB = ExitStack()
        dbc_sb = stackB.enter_context(nc.sbuf_tensor("dbc_sb", [96, TB], F32))
        dbc_r = stackB.enter_context(nc.sbuf_tensor("dbc_r", [96, 2 * TB], F32R))
        spu = stackB.enter_context(nc.sbuf_tensor("spu", [P, TB], F32))
        delta = stackB.enter_context(nc.sbuf_tensor("delta", [P, TB], F32))
        a_sb = stackB.enter_context(nc.sbuf_tensor("a_sb", [P, 2 * TB], F32))
        bx_sb = stackB.enter_context(nc.sbuf_tensor("bx_sb", [P, TB], F32))
        h_sb = stackB.enter_context(nc.sbuf_tensor("h_sb", [P, 2 * TB], F32))
        hc_sb = stackB.enter_context(nc.sbuf_tensor("hc_sb", [P, 2 * TB], F32))
        pool = [stackB.enter_context(nc.psum_tensor(f"pp{i}", [P, TB], F32))
                for i in range(4)]
        pool_guard = [(None, 0)] * 4
        pool_i = [0]

        def next_slot():
            s_ = pool_i[0] % 4
            pool_i[0] += 1
            return s_

        dbc_ag_v = dbc_ag.rearrange("(r p) t -> r p t", p=P)
        blk_order = []
        if 0 in _DEBUG_DIRS:
            blk_order += [(0, bi) for bi in range(NB)]
        if 1 in _DEBUG_DIRS:
            blk_order += [(1, bi) for bi in reversed(range(NB))]

        v_round = {}
        v_bx = {}
        v_hc = {}
        v_scan = {}
        v_yadd = {}
        g_iter = {}
        sc_iter = {}
        mm_end = {}
        sc_sp_prev = None
        v_u_prev = None
        NS = DSTATE

        def gs_of(idx, s):
            return idx * NS + s

        def by_gs(gs):
            return (gs // NS, gs % NS)

        for idx, (dr, bi) in enumerate(blk_order):
            pb = idx % 2
            rowbase = 64 * dr
            sender = bi // 2
            col0 = (bi % 2) * TB

            @block.sync
            def _(sync, pb=pb, rowbase=rowbase, sender=sender, col0=col0, idx=idx):
                if idx == 0:
                    sync.wait_ge(S_COLL, COLL_AG)
                if idx >= 1:
                    sync.wait_ge(S_V, v_round[idx - 1])
                dbcd.inc(sync.dma_start(
                    out=dbc_sb[0:DTR, :],
                    in_=dbc_ag_v[sender, rowbase:rowbase + DTR, col0:col0 + TB]))
                dbcd.inc(sync.dma_start(
                    out=dbc_sb[32:32 + DSTATE, :],
                    in_=dbc_ag_v[sender, rowbase + 32:rowbase + 48, col0:col0 + TB]))
                dbcd.inc(sync.dma_start(
                    out=dbc_sb[64:64 + DSTATE, :],
                    in_=dbc_ag_v[sender, rowbase + 48:rowbase + 64, col0:col0 + TB]))
            DBCD_T = dbcd.v

            @block.vector
            def _(v, pb=pb, idx=idx):
                v.wait_ge(S_DBCD, DBCD_T)
                if idx >= 2:
                    v.wait_ge(S_MM, mm_end[idx - 2])
                o = pb * TB
                v.tensor_copy(out=dbc_r[0:DTR, o:o + TB], in_=dbc_sb[0:DTR, :])
                v.tensor_copy(out=dbc_r[32:48, o:o + TB], in_=dbc_sb[32:48, :])
                vv.inc(v.tensor_copy(out=dbc_r[64:80, o:o + TB], in_=dbc_sb[64:80, :]))
            v_round[idx] = vv.v

            dslot = next_slot()
            @block.tensor
            def _(t, dr=dr, pb=pb, vt=vv.v, idx=idx, dslot=dslot, gd=pool_guard[dslot]):
                t.wait_ge(S_V, vt)
                if idx == 0:
                    t.wait_ge(S_SC, SC_ACOL)
                if gd[1]:
                    t.wait_ge(gd[0], gd[1])
                last = None
                for nk in range(TB // 512):
                    last = t.matmul(out=pool[dslot][:, nk * 512:(nk + 1) * 512],
                                    lhsT=dtwr[:, dr * P:(dr + 1) * P],
                                    rhs=dbc_r[0:DTR, pb * TB + nk * 512:pb * TB + (nk + 1) * 512],
                                    start=True, stop=True)
                mm.inc(last)
            MM_D = mm.v

            @block.scalar
            def _(s, dr=dr, idx=idx, dslot=dslot):
                s.wait_ge(S_MM, MM_D)
                if idx >= 1:
                    s.wait_ge(S_V, v_bx[gs_of(idx - 1, NS - 1)])
                s.activation(out=spu[:], in_=pool[dslot][:], func=AF.Exp,
                             bias=dtb_sb[:, dr:dr + 1])
                sc.inc(s.activation(out=delta[:], in_=spu[:], func=AF.Ln, bias=1.0))
            sc_sp_prev = sc.v
            pool_guard[dslot] = (S_SC, sc.v)

            @block.vector
            def _(v, bi=bi, idx=idx, sct=sc.v):
                v.wait_ge(S_SC, sct)
                if idx == 0:
                    v.wait_ge(sems["s_xstr"], 16)
                vv.inc(v.tensor_tensor(out=spu[:], in0=delta[:],
                                       in1=x_str[:, bi * TB:(bi + 1) * TB], op=ALU.mult))
            v_u_prev = vv.v

            for s in range(NS):
                sb = s % 2
                gs = gs_of(idx, s)
                sb2 = gs % 2

                bslot = next_slot()
                cslot = next_slot()
                @block.tensor
                def _(t, s=s, pb=pb, bslot=bslot, cslot=cslot,
                      gb=pool_guard[bslot], gc=pool_guard[cslot], vt=v_round[idx]):
                    t.wait_ge(S_V, vt)
                    if gb[1]:
                        t.wait_ge(gb[0], gb[1])
                    if gc[1]:
                        t.wait_ge(gc[0], gc[1])
                    last = None
                    for nk in range(TB // 512):
                        t.matmul(out=pool[bslot][:, nk * 512:(nk + 1) * 512],
                                 lhsT=selb16[32:48, s * P:(s + 1) * P],
                                 rhs=dbc_r[32:48, pb * TB + nk * 512:pb * TB + (nk + 1) * 512],
                                 start=True, stop=True)
                        last = t.matmul(out=pool[cslot][:, nk * 512:(nk + 1) * 512],
                                        lhsT=selb16[64:80, s * P:(s + 1) * P],
                                        rhs=dbc_r[64:80, pb * TB + nk * 512:pb * TB + (nk + 1) * 512],
                                        start=True, stop=True)
                    mm.inc(last)
                MM_BC = mm.v

                @block.scalar
                def _(s_, s=s, sb=sb, dr=dr, gs=gs):
                    s_.wait_ge(S_MM, MM_BC)
                    if gs >= 2:
                        s_.wait_ge(S_V, v_scan[gs - 2])
                    sc.inc(s_.activation(out=a_sb[:, sb * TB:(sb + 1) * TB],
                                         in_=delta[:], func=AF.Exp,
                                         scale=acol[:, dr * DSTATE + s:dr * DSTATE + s + 1]))
                sc_iter[gs] = sc.v

                @block.vector
                def _(v, s=s, sb=sb, sb2=sb2, dr=dr, bi=bi, idx=idx, gs=gs,
                      bslot=bslot, cslot=cslot, mt=MM_BC, sct=sc.v):
                    v.wait_ge(S_MM, mt)
                    v.wait_ge(S_SC, sct)
                    vv.inc(v.tensor_tensor(out=bx_sb[:], in0=spu[:],
                                           in1=pool[bslot][:], op=ALU.mult))
                    v_bx[gs] = vv.v
                    if dr == 1:
                        v.wait_ge(S_V, vv.v)
                    if gs >= 2:
                        v.wait_ge(S_G, g_iter[gs - 2])
                    first = (idx == 0) or (dr == 1 and blk_order[idx - 1][0] == 0)
                    ci = dr * DSTATE + s
                    init = 0.0 if first else chain[:, ci:ci + 1]
                    ho = sb2 * TB
                    if dr == 0:
                        ii = v.tensor_tensor_scan(out=h_sb[:, ho:ho + TB],
                                                  data0=a_sb[:, sb * TB:(sb + 1) * TB],
                                                  data1=bx_sb[:], initial=init,
                                                  op0=ALU.mult, op1=ALU.add)
                    else:
                        ii = v.tensor_tensor_scan(out=h_sb[:, ho:ho + TB],
                                                  data0=a_sb[:, sb * TB:(sb + 1) * TB][:, ::-1],
                                                  data1=bx_sb[:, ::-1], initial=init,
                                                  op0=ALU.mult, op1=ALU.add)
                    vv.inc(ii)
                    v_scan[gs] = vv.v
                    if dr == 1:
                        v.wait_ge(S_V, vv.v)
                    if gs >= 2 and (gs - 2) in v_hc:
                        v.wait_ge(S_G, g_iter[gs - 2])
                    if dr == 0:
                        ii = v.tensor_tensor(out=hc_sb[:, ho:ho + TB],
                                             in0=h_sb[:, ho:ho + TB],
                                             in1=pool[cslot][:], op=ALU.mult)
                    else:
                        ii = v.tensor_tensor(out=hc_sb[:, ho:ho + TB],
                                             in0=h_sb[:, ho:ho + TB][:, ::-1],
                                             in1=pool[cslot][:], op=ALU.mult)
                    vv.inc(ii)
                    v_hc[gs] = vv.v
                pool_guard[bslot] = (S_V, v_bx[gs])
                pool_guard[cslot] = (S_V, v_hc[gs])

                @block.gpsimd
                def _(g, sb2=sb2, dr=dr, bi=bi, s=s, gs=gs, vt=v_hc[gs]):
                    g.wait_ge(S_V, vt)
                    ho = sb2 * TB
                    ci = dr * DSTATE + s
                    if dr == 0:
                        g.tensor_copy(out=chain[:, ci:ci + 1],
                                      in_=h_sb[:, ho + TB - 1:ho + TB])
                        src = hc_sb[:, ho:ho + TB]
                    else:
                        g.tensor_copy(out=chain[:, ci:ci + 1],
                                      in_=h_sb[:, ho + TB - 1:ho + TB])
                        src = hc_sb[:, ho:ho + TB]
                    gg.inc(g.tensor_tensor(out=y_sb[:, bi * TB:(bi + 1) * TB],
                                           in0=y_sb[:, bi * TB:(bi + 1) * TB],
                                           in1=src, op=ALU.add))
                g_iter[gs] = gg.v

            mm_end[idx] = mm.v

        r_st = []
        for j in range(NCORES):
            rbuf = a_sb if j % 2 == 0 else h_sb

            @block.vector
            def _(v, j=j, rbuf=rbuf):
                if j == 0:
                    lastgs = gs_of(len(blk_order) - 1, NS - 1)
                    v.wait_ge(S_G, g_iter[lastgs])
                if j >= 2:
                    v.wait_ge(sems["s_r0" if j % 2 == 0 else "s_r1"], 16 * (j // 2))
                ii = None
                for q in range(2):
                    blk = slice(j * TC + q * TB, j * TC + (q + 1) * TB)
                    ii = v.scalar_tensor_tensor(out=rbuf[:, q * TB:(q + 1) * TB],
                                                in0=x_str[:, blk], scalar=dsum[:, 0:1],
                                                in1=y_sb[:, blk], op0=ALU.mult, op1=ALU.add)
                vv.inc(ii)

            @block.sync
            def _(sync, j=j, rbuf=rbuf, vt=vv.v):
                sync.wait_ge(S_V, vt)
                if stage == 1:
                    src = rbuf[:] if _DEBUG_DUMP == "r" else y_sb[:, j * TC:(j + 1) * TC]
                    st.inc(sync.dma_start(out=rdump_h[:, j * TC:(j + 1) * TC], in_=src))
                sync.dma_start(out=r_loc[:, j * TC:(j + 1) * TC],
                               in_=rbuf[:]).then_inc(
                                   sems["s_r0" if j % 2 == 0 else "s_r1"], 16)
            r_st.append(st.v)

        @block.gpsimd
        def _(g, stv=st.v):
            g.wait_ge(S_ST, stv)
            g.wait_ge(sems["s_r0"], 16 * 4)
            g.wait_ge(sems["s_r1"], 16 * 4)
            coll.inc(g.collective_compute("AllGather", ALU.bypass, replica_groups=RG,
                                          ins=[r_loc], outs=[rag_out]))
        COLL_A2A = coll.v

        stackB.close()
        stackAB.close()

        if stage == 1:
            @block.sync
            def _(sync):
                sync.wait_ge(S_COLL, COLL_A2A)
                sync.wait_ge(S_ST, st.v)
            return nc

        stackD = ExitStack()
        NGRAN = 4
        wld = stackD.enter_context(nc.sbuf_tensor("wld", [P, NGRAN * 1024], F32))
        wbf = stackD.enter_context(nc.sbuf_tensor("wbf", [P, NGRAN * 1024], BF16))
        sact = stackD.enter_context(nc.sbuf_tensor("sact", [P, TH], F32))
        lgsm = stackD.enter_context(nc.sbuf_tensor("lgsm", [E, TH], F32))
        mskf = stackD.enter_context(nc.sbuf_tensor("mskf", [E, TH], F32))
        cnt = stackD.enter_context(nc.sbuf_tensor("cnt", [E, TH], F32))
        stat = stackD.enter_context(nc.sbuf_tensor("stat", [P, 24], F32))
        lgtm = stackD.enter_context(nc.sbuf_tensor("lgtm", [P, NT * E], F32))
        smtm = stackD.enter_context(nc.sbuf_tensor("smtm", [P, NT * E], F32))
        cnttm = stackD.enter_context(nc.sbuf_tensor("cnttm", [P, NT * E], F32))
        rsc = stackD.enter_context(nc.sbuf_tensor("rsc", [P, NT * E], F32))
        e4c = stackD.enter_context(nc.sbuf_tensor("e4c", [P, NT * E], F32))
        iota_t = stackD.enter_context(nc.sbuf_tensor("iota_t", [P, NT], I32))
        pref_t = stackD.enter_context(nc.sbuf_tensor("pref_t", [P, NTS], I32))
        idx_sct = stackD.enter_context(nc.sbuf_tensor("idx_sct", [P, NT], I32))
        idx_g = stackD.enter_context(nc.sbuf_tensor("idx_g", [P, NTS], I32))
        sm_slot = stackD.enter_context(nc.sbuf_tensor("sm_slot", [P, NTS], F32))
        smv = stackD.enter_context(nc.sbuf_tensor("smv", [16, P], F32))
        act_cmp = stackD.enter_context(nc.sbuf_tensor("act_cmp", [P, IT * CAP], BF16))
        yc = stackD.enter_context(nc.sbuf_tensor("yc", [P, 8 * 512], F32))
        ps_gu = [stackD.enter_context(nc.psum_tensor(f"ps_gu{i}", [P, 512], F32))
                 for i in range(4)]
        ps_y = stackD.enter_context(nc.psum_tensor("ps_y", [P, 1024], F32))
        ps_ty = stackD.enter_context(nc.psum_tensor("ps_ty", [P, 1024], F32))

        r_fm = x_str[:, 0:8 * TC]
        x1_tm = y_sb[:, 0:NT * H]
        x1_bf = x_str[:, 0:4 * TC].bitcast(BF16)
        act_sh = x_str[:, 4 * TC:8 * TC].bitcast(BF16)
        xs_fm = x_str[:, 4 * TC:6 * TC].bitcast(BF16)
        xg = x_str[:, 6 * TC:6 * TC + 1024]
        xs_tok = x_str[:, 6 * TC + 1024:6 * TC + 2048].bitcast(BF16)
        y_tok = x_str[:, 7 * TC:8 * TC]
        r2 = x_str[:, 0:NT * H]

        fm = Ctr(S_FFM)
        fv = Ctr(S_FFV)
        fs = Ctr(S_FFS)
        wd = Ctr(S_WD, 16)
        gx = Ctr(sems["s_xc0"], 16)
        gx.v = 64
        gsm = Ctr(sems["s_xc1"], 16)
        gsm.v = 64
        ysc = Ctr(sems["s_r0"], 16)
        ysc.v = 64
        yl0 = Ctr(sems["s_r1"], 16)
        yl0.v = 64
        yl1 = Ctr(S_LD, 16)
        yl1.v = LD_ALL

        def pst_of(j):
            return ps_ty[:, (j % 2) * 512:(j % 2) * 512 + P]

        @block.sync
        def _(sync):
            sync.wait_ge(S_COLL, COLL_A2A)
            wd.inc(sync.dma_start(out=wld[:, 0:HC * E].rearrange("p (c m) -> p c m", c=HC),
                                  in_=gw_h.rearrange("(c p) m -> p c m", p=P)))

        @block.vector
        def _(v, wt=wd.v):
            v.wait_ge(S_WD, wt)
            fv.inc(v.tensor_copy(out=gw_bf[:], in_=wld[:, 0:HC * E]))
        FV_GW = fv.v

        @block.gpsimd
        def _(g):
            g.wait_ge(S_COLL, COLL_A2A)
            g.iota(out=iota_t[:], pattern=[[P, NT]], base=0, channel_multiplier=1)
            g.iota(out=pref_t[:], pattern=[[0, NTS]], base=TC, channel_multiplier=0)
            g.iota(out=e4c[:], pattern=[[0, NT], [4, E]], base=0, channel_multiplier=0,
                   allow_small_or_imprecise_dtypes=True)
            g.memset(cnt[:], 0.0)
            dbcd.inc(g.dma_start(out=tok_dram.rearrange("(p f) o -> p f o", p=P),
                                 in_=pref_t[:].rearrange("p (f o) -> p f o", o=1)))
            dbcd.inc(g.dma_start(out=x1_dram[TC:TC + 1, :], in_=cnt[0:1, :]))
            dbcd.inc(g.dma_start(out=sm_dram[TC:TC + 1, :], in_=cnt[0:1, 0:1]))
        TOK_PREF = dbcd.v

        rag_v = rag_out.rearrange("(c p) t -> p c t", p=P)

        wsems = [sems[f"s_w{i}"] for i in range(NGRAN)]
        wld_free = [0] * NGRAN
        wld_free_sem = [None] * NGRAN
        wld_free_sem[0] = S_FFV
        wld_free[0] = FV_GW
        wld_uses = [0] * NGRAN
        wbf_free = [0] * NGRAN
        ring = {"w": 0, "b": 0}

        def emit_wload(src_ap, ngran):
            while True:
                cand = ring["w"] % NGRAN
                if ngran == 2 and cand % 2 == 1:
                    ring["w"] += 1
                    continue
                g0 = cand
                ring["w"] += ngran
                break
            waits = [(wld_free_sem[g0 + k], wld_free[g0 + k]) for k in range(ngran)]
            n_tr = 4 * ngran

            @block.sync
            def _(sync, g0=g0, src_ap=src_ap, ngran=ngran, waits=waits, n_tr=n_tr):
                for sem_, wt in waits:
                    if wt:
                        sync.wait_ge(sem_, wt)
                dst = wld[:, g0 * 1024:(g0 + ngran) * 1024].rearrange(
                    "p (c m) -> p c m", m=P)
                for j in range(n_tr):
                    sync.dma_start(out=dst[:, 2 * j:2 * j + 2],
                                   in_=src_ap[:, 2 * j:2 * j + 2]).then_inc(
                                       wsems[g0], 16)
            wld_uses[g0] += 16 * n_tr
            mark = (wsems[g0], wld_uses[g0])
            return g0, mark

        WC_ROT = ["v", "g"]
        wc_i = [0]

        def emit_wcast(g0, wt, ngran):
            eng = WC_ROT[wc_i[0] % len(WC_ROT)]
            wc_i[0] += 1
            while True:
                cand = ring["b"] % NGRAN
                if ngran == 2 and cand % 2 == 1:
                    ring["b"] += 1
                    continue
                b0 = cand
                ring["b"] += ngran
                break
            waits = [wbf_free[b0 + k] for k in range(ngran)]

            def body(e_):
                e_.wait_ge(wt[0], wt[1])
                for w_ in waits:
                    if w_:
                        e_.wait_ge(S_FFM, w_)

            if eng == "v":
                @block.vector
                def _(v):
                    body(v)
                    fv.inc(v.tensor_copy(out=wbf[:, b0 * 1024:(b0 + ngran) * 1024],
                                         in_=wld[:, g0 * 1024:(g0 + ngran) * 1024]))
                mark_sem, mark = S_FFV, fv.v
            elif eng == "s":
                @block.scalar
                def _(s):
                    body(s)
                    fs.inc(s.copy(out=wbf[:, b0 * 1024:(b0 + ngran) * 1024],
                                  in_=wld[:, g0 * 1024:(g0 + ngran) * 1024]))
                mark_sem, mark = S_FFS, fs.v
            else:
                @block.gpsimd
                def _(g):
                    body(g)
                    gg.inc(g.tensor_copy(out=wbf[:, b0 * 1024:(b0 + ngran) * 1024],
                                         in_=wld[:, g0 * 1024:(g0 + ngran) * 1024]))
                mark_sem, mark = S_G, gg.v
            for k in range(ngran):
                wld_free_sem[g0 + k] = mark_sem
                wld_free[g0 + k] = mark
            return b0, (mark_sem, mark)

        @block.sync
        def _(sync):
            sync.wait_ge(S_COLL, COLL_A2A)
            pid = nc.partition_id([mybir.EngineType.SP])
            for c in range(HC):
                sync.dma_start(out=r_fm[:, c * TC:(c + 1) * TC],
                               in_=rag_v[:, c, bass.ts(pid, TC)]).then_inc(
                                   sems["s_rl"], 16)

        s_tp = {}
        n_tp = 0
        for c in range(HC):
            for tt in range(NT):
                @block.tensor
                def _(t, c=c, tt=tt, n_tp=n_tp):
                    if n_tp == 0:
                        t.wait_ge(sems["s_rl"], 16 * HC)
                    if n_tp >= 2:
                        t.wait_ge(S_FFV, s_tp[n_tp - 2])
                    fm.inc(t.transpose(out=pst_of(n_tp),
                                       in_=r_fm[:, c * TC + tt * P:c * TC + (tt + 1) * P],
                                       identity=ident[:]))

                @block.vector
                def _(v, c=c, tt=tt, n_tp=n_tp, mt=fm.v):
                    v.wait_ge(S_FFM, mt)
                    fv.inc(v.tensor_copy(out=x1_tm[:, tt * H + c * P:tt * H + (c + 1) * P],
                                         in_=pst_of(n_tp)))
                s_tp[n_tp] = fv.v
                n_tp += 1
        FS_RTM = fv.v

        def emit_ln(buf, tt, first_wait):
            @block.vector
            def _(v, tt=tt, fw=first_wait):
                if fw is not None:
                    v.wait_ge(fw[0], fw[1])
                fv.inc(v.tensor_tensor(out=sact[:], in0=buf[:, tt * H:(tt + 1) * H],
                                       in1=buf[:, tt * H:(tt + 1) * H], op=ALU.mult))
                v.wait_ge(S_FFV, fv.v)
                v.reduce_sum(out=stat[:, 0:1], in_=buf[:, tt * H:(tt + 1) * H], axis=AX.X)
                fv.inc(v.reduce_sum(out=stat[:, 1:2], in_=sact[:], axis=AX.X))
                v.wait_ge(S_FFV, fv.v)
                v.tensor_scalar_mul(out=stat[:, 0:1], in0=stat[:, 0:1], scalar1=1.0 / H)
                fv.inc(v.tensor_scalar_mul(out=stat[:, 1:2], in0=stat[:, 1:2], scalar1=1.0 / H))
                v.wait_ge(S_FFV, fv.v)
                fv.inc(v.tensor_tensor(out=stat[:, 2:3], in0=stat[:, 0:1],
                                       in1=stat[:, 0:1], op=ALU.mult))
                v.wait_ge(S_FFV, fv.v)
                fv.inc(v.tensor_tensor(out=stat[:, 1:2], in0=stat[:, 1:2],
                                       in1=stat[:, 2:3], op=ALU.subtract))
                v.wait_ge(S_FFV, fv.v)
                fv.inc(v.tensor_scalar_add(out=stat[:, 1:2], in0=stat[:, 1:2],
                                           scalar1=1e-5))
                v.wait_ge(S_FFV, fv.v)
                fv.inc(v.reciprocal(out=stat[:, 2:3], in_=stat[:, 1:2]))

            @block.scalar
            def _(s, vt=fv.v):
                s.wait_ge(S_FFV, vt)
                fs.inc(s.sqrt(out=stat[:, 3:4], in_=stat[:, 2:3]))

            @block.vector
            def _(v, tt=tt, st_=fs.v):
                v.wait_ge(S_FFS, st_)
                fv.inc(v.tensor_scalar(out=buf[:, tt * H:(tt + 1) * H],
                                       in0=buf[:, tt * H:(tt + 1) * H],
                                       scalar1=stat[:, 0:1], scalar2=stat[:, 3:4],
                                       op0=ALU.subtract, op1=ALU.mult))
            return fv.v

        for tt in range(NT):
            emit_ln(x1_tm, tt, (S_FFV, FS_RTM) if tt == 0 else None)
        FV_LN1 = fv.v

        @block.sync
        def _(sync, vt=FV_LN1):
            sync.wait_ge(S_FFV, vt)
            for qq in range(4):
                dbcd.inc(sync.dma_start(
                    out=x1_dram[qq * 512:(qq + 1) * 512, :].rearrange(
                        "(tt p) f -> p tt f", p=P),
                    in_=x1_tm[:, qq * 4 * H:(qq + 1) * 4 * H].rearrange(
                        "p (tt f) -> p tt f", tt=4)))
        TOK_X1 = dbcd.v

        s_tp2 = {}
        n_tp2 = 0
        for c in range(HC):
            for tt in range(NT):
                @block.tensor
                def _(t, c=c, tt=tt, n_tp2=n_tp2):
                    if n_tp2 == 0:
                        t.wait_ge(S_FFV, FV_LN1)
                    if n_tp2 >= 2:
                        t.wait_ge(S_FFV, s_tp2[n_tp2 - 2])
                    fm.inc(t.transpose(out=pst_of(n_tp2),
                                       in_=x1_tm[:, tt * H + c * P:tt * H + (c + 1) * P],
                                       identity=ident[:]))

                @block.vector
                def _(v, c=c, tt=tt, n_tp2=n_tp2, mt=fm.v):
                    v.wait_ge(S_FFM, mt)
                    fv.inc(v.tensor_copy(out=x1_bf[:, c * TC + tt * P:c * TC + (tt + 1) * P],
                                         in_=pst_of(n_tp2)))
                s_tp2[n_tp2] = fv.v
                n_tp2 += 1
        FS_X1BF = fv.v

        v_lgt = {}
        FS_LG = 0
        for lp in range(2):
            @block.tensor
            def _(t, lp=lp, prev=FS_LG):
                t.wait_ge(S_FFV, FS_X1BF)
                t.wait_ge(S_FFV, FV_GW)
                if lp == 1:
                    t.wait_ge(S_FFV, prev)
                last = None
                for c in range(HC):
                    for nk in range(2):
                        last = t.matmul(
                            out=ps_y[0:E, nk * 512:(nk + 1) * 512],
                            lhsT=gw_bf[:, c * E:(c + 1) * E],
                            rhs=x1_bf[:, c * TC + lp * TH + nk * 512:
                                      c * TC + lp * TH + (nk + 1) * 512],
                            start=(c == 0), stop=(c == HC - 1))
                fm.inc(last)

            @block.vector
            def _(v, lp=lp, mt=fm.v):
                v.wait_ge(S_FFM, mt)
                fv.inc(v.tensor_copy(out=lgsm[:], in_=ps_y[0:E, :]))
            FS_LG = fv.v

            for ttl in range(8):
                tt = lp * 8 + ttl

                @block.tensor
                def _(t, ttl=ttl, tt=tt, lg=FS_LG):
                    if ttl == 0:
                        t.wait_ge(S_FFV, lg)
                    if tt >= 2:
                        t.wait_ge(S_FFV, v_lgt[tt - 2])
                    fm.inc(t.transpose(out=pst_of(tt)[0:P, 0:E],
                                       in_=lgsm[:, ttl * P:(ttl + 1) * P],
                                       identity=ident[0:E, 0:E]))

                @block.vector
                def _(v, tt=tt, mt=fm.v):
                    v.wait_ge(S_FFM, mt)
                    fv.inc(v.tensor_copy(out=lgtm[:, tt * E:(tt + 1) * E],
                                         in_=pst_of(tt)[0:P, 0:E]))
                v_lgt[tt] = fv.v
        psy_guard = [(S_FFV, FS_LG), (S_FFV, FS_LG)]

        for tt in range(NT):
            @block.vector
            def _(v, tt=tt, vt=v_lgt[NT - 1]):
                if tt == 0:
                    v.wait_ge(S_FFV, vt)
                fv.inc(v.reduce_max(out=stat[:, 4 + tt:5 + tt],
                                    in_=lgtm[:, tt * E:(tt + 1) * E], axis=AX.X))
                v.wait_ge(S_FFV, fv.v)
                fv.inc(v.tensor_single_scalar(out=smtm[:, tt * E:(tt + 1) * E],
                                              in_=lgtm[:, tt * E:(tt + 1) * E],
                                              scalar=stat[:, 4 + tt:5 + tt],
                                              op=ALU.is_equal))
        FV_EQ = fv.v

        @block.scalar
        def _(s, vt=FV_EQ):
            s.wait_ge(S_FFV, vt)
            fs.inc(s.activation(out=stat[:, 4:20], in_=stat[:, 4:20], func=AF.Sigmoid))
        FS_SIG = fs.v

        tp_hist = {-1: (S_FFV, FV_EQ)}
        tp_i = [1]
        v_scan = [0]
        fm_cntt = [0]

        @block.tensor
        def _(t, sg=FS_SIG, lg=v_lgt[NT - 2]):
            t.wait_ge(S_FFS, sg)
            t.wait_ge(S_FFV, lg)
            fm.inc(t.transpose(out=pst_of(0)[0:16, 0:P], in_=stat[:, 4:20],
                               identity=ident[:]))

        @block.vector
        def _(v, mt=fm.v):
            v.wait_ge(S_FFM, mt)
            fv.inc(v.tensor_copy(out=smv[:], in_=pst_of(0)[0:16, 0:P]))
        tp_hist[0] = (S_FFV, fv.v)

        @block.sync
        def _(sync, vt=fv.v):
            sync.wait_ge(S_FFV, vt)
            dbcd.inc(sync.dma_start(
                out=sm_dram[0:TC, :].rearrange("(a b) o -> a (b o)", a=16),
                in_=smv[:]))
        TOK_SM = dbcd.v
        for q in range(2):
            for ttl in range(8):
                tt = q * 8 + ttl
                j = tp_i[0]
                tp_i[0] += 1

                @block.tensor
                def _(t, ttl=ttl, tt=tt, j=j):
                    hw = tp_hist.get(j - 2)
                    if hw is not None:
                        t.wait_ge(hw[0], hw[1])
                    fm.inc(t.transpose(out=pst_of(j)[0:E, 0:P],
                                       in_=smtm[:, tt * E:(tt + 1) * E],
                                       identity=ident[:]))

                @block.vector
                def _(v, ttl=ttl, j=j, q=q, mt=fm.v, vs=v_scan[0]):
                    v.wait_ge(S_FFM, mt)
                    if q == 1 and ttl == 0:
                        v.wait_ge(S_FFV, vs)
                    fv.inc(v.tensor_copy(out=mskf[:, ttl * P:(ttl + 1) * P],
                                         in_=pst_of(j)[0:E, 0:P]))
                tp_hist[j] = (S_FFV, fv.v)
            FS_MSK = fv.v

            @block.vector
            def _(v, q=q, st_=FS_MSK, mt=fm_cntt[0]):
                v.wait_ge(S_FFV, st_)
                v.wait_ge(S_DBCD, TOK_PREF)
                if q == 1:
                    v.wait_ge(S_FFM, mt)
                    fv.inc(v.tensor_copy(out=stat[0:E, 23:24], in_=cnt[:, TH - 1:TH]))
                    v.wait_ge(S_FFV, fv.v)
                fv.inc(v.tensor_tensor_scan(out=cnt[:], data0=mskf[:], data1=mskf[:],
                                            initial=0.0 if q == 0 else stat[0:E, 23:24],
                                            op0=ALU.add, op1=ALU.add))
            v_scan[0] = fv.v

            for ttl in range(8):
                tt = q * 8 + ttl
                j = tp_i[0]
                tp_i[0] += 1

                @block.tensor
                def _(t, ttl=ttl, j=j, vt=v_scan[0]):
                    if ttl == 0:
                        t.wait_ge(S_FFV, vt)
                    hw = tp_hist[j - 2]
                    t.wait_ge(hw[0], hw[1])
                    fm.inc(t.transpose(out=pst_of(j)[0:P, 0:E],
                                       in_=cnt[:, ttl * P:(ttl + 1) * P],
                                       identity=ident[0:E, 0:E]))

                @block.vector
                def _(v, tt=tt, j=j, mt=fm.v):
                    v.wait_ge(S_FFM, mt)
                    fv.inc(v.tensor_copy(out=cnttm[:, tt * E:(tt + 1) * E],
                                         in_=pst_of(j)[0:P, 0:E]))
                tp_hist[j] = (S_FFV, fv.v)
            fm_cntt[0] = fm.v
        FV_CNTT = fv.v

        @block.vector
        def _(v):
            v.wait_ge(S_FFV, FV_CNTT)
            fv.inc(v.tensor_scalar(out=rsc[:], in0=cnttm[:], scalar1=0.5,
                                   scalar2=-1.0, op0=ALU.mult, op1=ALU.add))
            v.wait_ge(S_FFV, fv.v)
            fv.inc(v.tensor_scalar(out=rsc[:], in0=rsc[:], scalar1=511.0,
                                   scalar2=None, op0=ALU.min))
            v.wait_ge(S_FFV, fv.v)
            fv.inc(v.tensor_scalar(out=cnttm[:], in0=rsc[:], scalar1=128.0,
                                   scalar2=None, op0=ALU.is_ge))
            fv.inc(v.tensor_scalar(out=lgtm[:], in0=rsc[:], scalar1=256.0,
                                   scalar2=None, op0=ALU.is_ge))
            v.wait_ge(S_FFV, fv.v)
            fv.inc(v.tensor_tensor(out=cnttm[:], in0=cnttm[:], in1=lgtm[:], op=ALU.add))
            fv.inc(v.tensor_scalar(out=lgtm[:], in0=rsc[:], scalar1=384.0,
                                   scalar2=None, op0=ALU.is_ge))
            v.wait_ge(S_FFV, fv.v)
            fv.inc(v.tensor_tensor(out=cnttm[:], in0=cnttm[:], in1=lgtm[:], op=ALU.add))
            fv.inc(v.tensor_scalar(out=rsc[:], in0=rsc[:], scalar1=32.0,
                                   scalar2=None, op0=ALU.mult))
            v.wait_ge(S_FFV, fv.v)
            fv.inc(v.scalar_tensor_tensor(out=rsc[:], in0=cnttm[:], scalar=-4095.0,
                                          in1=rsc[:], op0=ALU.mult, op1=ALU.add))
            v.wait_ge(S_FFV, fv.v)
            fv.inc(v.tensor_tensor(out=rsc[:], in0=rsc[:], in1=e4c[:], op=ALU.add))
            v.wait_ge(S_FFV, fv.v)
            fv.inc(v.tensor_tensor(out=rsc[:], in0=rsc[:], in1=smtm[:], op=ALU.mult))
            v.wait_ge(S_FFV, fv.v)
            for tt in range(NT):
                fv.inc(v.reduce_sum(out=smtm[:, tt:tt + 1],
                                    in_=rsc[:, tt * E:(tt + 1) * E], axis=AX.X))
            v.wait_ge(S_FFV, fv.v)
            fv.inc(v.tensor_copy(out=idx_sct[:], in_=smtm[:, 0:NT]))
        FV_IDX = fv.v

        slot_guard_s = [0, 0]
        slot_guard_v = [0, 0]
        ty_hist = {-2: (S_FFV, FV_CNTT), -1: (S_FFV, FV_CNTT)}
        ty_i = [0]

        def ty_half():
            j = ty_i[0]
            ty_i[0] += 1
            return j, ps_ty[:, (j % 2) * 512:(j % 2 + 1) * 512], ty_hist[j - 2]
        yc_free = (None, 0)
        FM_SHFREE = 0
        prem_prev = 0
        xs_ready = {}
        xs_done_d = {}
        ysc_k = [0]
        TOK_IDX = 0
        GSM_ALL = gsm.v

        def emit_prep(e):
            nonlocal prem_prev
            sl = e % 2
            for tl in range(4):
                tau = 4 * e + tl

                @block.gpsimd
                def _(g, tau=tau, e=e, tl=tl, pp=prem_prev, ti=TOK_IDX,
                      shf=FM_SHFREE):
                    if e == 0 and tl == 0:
                        g.wait_ge(S_DBCD, ti)
                        g.wait_ge(S_FFM, shf)
                    if pp:
                        g.wait_ge(S_FFV, pp)
                    gx.inc(g.indirect_dma_start(
                        out=xg[:], out_offset=None,
                        in_=x1_dram[:],
                        in_offset=IOA(ap=idx_g[:, tau:tau + 1], axis=0)))

                @block.vector
                def _(v, tau=tau, gxv=gx.v, gall=GSM_ALL):
                    v.wait_ge(sems["s_xc0"], gxv)
                    v.wait_ge(sems["s_xc1"], gall)
                    fv.inc(v.tensor_scalar(
                        out=xs_tok[:, (tau % 2) * 1024:(tau % 2 + 1) * 1024],
                        in0=xg[:], scalar1=sm_slot[:, tau:tau + 1],
                        scalar2=None, op0=ALU.mult))
                prem_prev = fv.v

                for hg in range(2):
                    tyj, tyh, tyw = ty_half()

                    @block.tensor
                    def _(t, tau=tau, hg=hg, vt=fv.v, tyh=tyh, tyw=tyw):
                        if hg == 0:
                            t.wait_ge(S_FFV, vt)
                        t.wait_ge(tyw[0], tyw[1])
                        last = None
                        tyb = tyh.bitcast(BF16)
                        for cl in range(4):
                            c = hg * 4 + cl
                            last = t.transpose(
                                out=tyb[:, cl * P:(cl + 1) * P],
                                in_=xs_tok[:, (tau % 2) * 1024 + c * P:
                                           (tau % 2) * 1024 + (c + 1) * P],
                                identity=identb[:])
                        fm.inc(last)

                    @block.vector
                    def _(v, tau=tau, tl=tl, sl=sl, hg=hg, mt=fm.v, tyh=tyh,
                          xr=xs_ready.get(e - 2, 0)):
                        v.wait_ge(S_FFM, mt)
                        if tl == 0 and hg == 0 and xr:
                            v.wait_ge(S_FFM, xr)
                        fv.inc(v.tensor_copy(
                            out=xs_fm[:, sl * 4096:(sl + 1) * 4096].rearrange(
                                "p (c s) -> p c s", s=CAP)[
                                :, hg * 4:(hg + 1) * 4, tl * P:(tl + 1) * P],
                            in_=tyh.bitcast(BF16)[:, 0:4 * P].rearrange(
                                "p (c s) -> p c s", s=P)))
                    ty_hist[tyj] = (S_FFV, fv.v)
                xs_done_d[e] = fv.v

        for ei in range(9):
            if ei == 0:
                for vh in range(2):
                    for it in range(IT):
                        g0, wt0 = emit_wload(shg_h[it], 1)
                        bg, (bgs, bgv) = emit_wcast(g0, wt0, 1)
                        g1, wt1 = emit_wload(shu_h[it], 1)
                        bu, (bus, buv) = emit_wcast(g1, wt1, 1)

                        @block.tensor
                        def _(t, it=it, vh=vh, bg=bg, bu=bu, bgs=bgs, bgv=bgv,
                              bus=bus, buv=buv,
                              sg=list(slot_guard_s), sv=list(slot_guard_v)):
                            t.wait_ge(bgs, bgv)
                            t.wait_ge(bus, buv)
                            if it == 0 and vh == 0:
                                t.wait_ge(S_FFV, FS_X1BF)
                            for nk in range(2):
                                if sg[nk]:
                                    t.wait_ge(S_FFS, sg[nk])
                                if sv[nk]:
                                    t.wait_ge(S_FFV, sv[nk])
                            last = None
                            for c in range(HC):
                                for nk in range(2):
                                    last = t.matmul(
                                        out=ps_gu[nk][:],
                                        lhsT=wbf[:, bg * 1024 + c * P:bg * 1024 + (c + 1) * P],
                                        rhs=x1_bf[:, c * TC + vh * TH + nk * 512:
                                                  c * TC + vh * TH + (nk + 1) * 512],
                                        start=(c == 0), stop=(c == HC - 1))
                            fm.inc(last)
                            for c in range(HC):
                                for nk in range(2):
                                    last = t.matmul(
                                        out=ps_gu[2 + nk][:],
                                        lhsT=wbf[:, bu * 1024 + c * P:bu * 1024 + (c + 1) * P],
                                        rhs=x1_bf[:, c * TC + vh * TH + nk * 512:
                                                  c * TC + vh * TH + (nk + 1) * 512],
                                        start=(c == 0), stop=(c == HC - 1))
                            fm.inc(last)
                        FM_G = fm.v - 1
                        FM_GU = fm.v
                        wbf_free[bg] = fm.v
                        wbf_free[bu] = fm.v
                        for nk in range(2):
                            @block.scalar
                            def _(s, nk=nk, mt=FM_G):
                                s.wait_ge(S_FFM, mt)
                                fs.inc(s.activation(out=sact[:, nk * 512:(nk + 1) * 512],
                                                    in_=ps_gu[nk][:], func=AF.Silu))
                            slot_guard_s[nk] = fs.v

                            @block.vector
                            def _(v, it=it, nk=nk, st_=fs.v, mt=FM_GU):
                                v.wait_ge(S_FFS, st_)
                                v.wait_ge(S_FFM, mt)
                                fv.inc(v.tensor_tensor(
                                    out=act_sh[:, it * TH + nk * 512:it * TH + (nk + 1) * 512],
                                    in0=sact[:, nk * 512:(nk + 1) * 512],
                                    in1=ps_gu[2 + nk][:], op=ALU.mult))
                            slot_guard_v[nk] = fv.v
                    FV_ACTSH = fv.v

                    for ht in range(HC):
                        gd, wtd = emit_wload(shd_h[ht], 2)
                        bd, (cts, ctv) = emit_wcast(gd, wtd, 2)

                        @block.tensor
                        def _(t, ht=ht, bd=bd, cts=cts, ctv=ctv, at=FV_ACTSH,
                              pg=list(psy_guard)):
                            t.wait_ge(cts, ctv)
                            t.wait_ge(S_FFV, at)
                            for hh in range(2):
                                if pg[hh][1]:
                                    t.wait_ge(pg[hh][0], pg[hh][1])
                            last = None
                            for ic in range(IT):
                                for nk in range(2):
                                    last = t.matmul(
                                        out=ps_y[:, nk * 512:(nk + 1) * 512],
                                        lhsT=wbf[:, bd * 1024 + ic * P:bd * 1024 + (ic + 1) * P],
                                        rhs=act_sh[:, ic * TH + nk * 512:ic * TH + (nk + 1) * 512],
                                        start=(ic == 0), stop=(ic == IT - 1))
                            fm.inc(last)
                        wbf_free[bd] = fm.v
                        wbf_free[bd + 1] = fm.v
                        FM_DN = fm.v

                        @block.vector
                        def _(v, ht=ht, mt=FM_DN, ycf=yc_free):
                            v.wait_ge(S_FFM, mt)
                            if ycf[1]:
                                v.wait_ge(ycf[0], ycf[1])
                            fv.inc(v.tensor_copy(out=yc[:, (ht % 2) * 1024:(ht % 2 + 1) * 1024],
                                                 in_=ps_y[:]))
                        FS_YSH = fv.v
                        psy_guard[0] = (S_FFV, fv.v)
                        psy_guard[1] = (S_FFV, fv.v)

                        for hg in range(2):
                            tyj, tyh, tyw = ty_half()

                            @block.tensor
                            def _(t, ht=ht, hg=hg, st_=FS_YSH, tyh=tyh, tyw=tyw):
                                if hg == 0:
                                    t.wait_ge(S_FFV, st_)
                                t.wait_ge(tyw[0], tyw[1])
                                last = None
                                for tl_ in range(4):
                                    ttl = hg * 4 + tl_
                                    last = t.transpose(
                                        out=tyh[:, tl_ * P:(tl_ + 1) * P],
                                        in_=yc[:, (ht % 2) * 1024 + ttl * P:
                                               (ht % 2) * 1024 + (ttl + 1) * P],
                                        identity=ident[:])
                                fm.inc(last)

                            @block.vector
                            def _(v, ht=ht, vh=vh, hg=hg, mt=fm.v, x1st=TOK_X1,
                                  tyh=tyh):
                                v.wait_ge(S_FFM, mt)
                                if ht == 0 and vh == 0 and hg == 0:
                                    v.wait_ge(S_DBCD, x1st)
                                base = vh * 8 + hg * 4
                                fv.inc(v.tensor_tensor(
                                    out=x1_tm.rearrange("p (tt f) -> p tt f", f=H)[
                                        :, base:base + 4, ht * P:(ht + 1) * P],
                                    in0=x1_tm.rearrange("p (tt f) -> p tt f", f=H)[
                                        :, base:base + 4, ht * P:(ht + 1) * P],
                                    in1=tyh.rearrange("p (tt f) -> p tt f", f=P),
                                    op=ALU.add))
                            ty_hist[tyj] = (S_FFV, fv.v)
                        yc_free = (S_FFV, fv.v)
                FM_SHFREE = fm.v
            else:
                e = ei - 1
                sl = e % 2
                if ei == 1:
                    @block.gpsimd
                    def _(g):
                        g.wait_ge(S_FFV, FV_IDX)
                        g.wait_ge(S_DBCD, TOK_PREF)
                        for tt in range(NT):
                            dbcd.inc(g.indirect_dma_start(
                                out=tok_dram[:],
                                out_offset=IOA(ap=idx_sct[:, tt:tt + 1], axis=0),
                                in_=iota_t[:, tt:tt + 1], in_offset=None))
                    TOK_SCT = dbcd.v

                    @block.gpsimd
                    def _(g):
                        g.wait_ge(S_DBCD, TOK_SCT)
                        dbcd.inc(g.dma_start(
                            out=idx_g[:].rearrange("p (f o) -> p f o", o=1),
                            in_=tok_dram.rearrange("(p f) o -> p f o", p=P)))
                    TOK_IDX = dbcd.v

                    @block.gpsimd
                    def _(g):
                        g.wait_ge(S_DBCD, TOK_IDX)
                        for j in range(NTS):
                            gsm.inc(g.indirect_dma_start(
                                out=sm_slot[:, j:j + 1], out_offset=None,
                                in_=sm_dram[:],
                                in_offset=IOA(ap=idx_g[:, j:j + 1], axis=0)))
                    GSM_ALL = gsm.v

                if e == 0:
                    emit_prep(0)
                xs_done = xs_done_d[e]

                for it in range(IT):
                    g0, wt0 = emit_wload(rtg_h[e][it], 1)
                    bg, (bgs, bgv) = emit_wcast(g0, wt0, 1)
                    g1, wt1 = emit_wload(rtu_h[e][it], 1)
                    bu, (bus, buv) = emit_wcast(g1, wt1, 1)

                    @block.tensor
                    def _(t, it=it, sl=sl, bg=bg, bu=bu, bgs=bgs, bgv=bgv,
                          bus=bus, buv=buv, xsd=xs_done,
                          sg=slot_guard_s[it % 2], sv=slot_guard_v[it % 2]):
                        t.wait_ge(bgs, bgv)
                        t.wait_ge(bus, buv)
                        t.wait_ge(S_FFV, xsd)
                        if sg:
                            t.wait_ge(S_FFS, sg)
                        if sv:
                            t.wait_ge(S_FFV, sv)
                        last = None
                        for c in range(HC):
                            last = t.matmul(
                                out=ps_gu[it % 2][:],
                                lhsT=wbf[:, bg * 1024 + c * P:bg * 1024 + (c + 1) * P],
                                rhs=xs_fm[:, sl * 4096 + c * CAP:sl * 4096 + (c + 1) * CAP],
                                start=(c == 0), stop=(c == HC - 1))
                        fm.inc(last)
                        for c in range(HC):
                            last = t.matmul(
                                out=ps_gu[2 + it % 2][:],
                                lhsT=wbf[:, bu * 1024 + c * P:bu * 1024 + (c + 1) * P],
                                rhs=xs_fm[:, sl * 4096 + c * CAP:sl * 4096 + (c + 1) * CAP],
                                start=(c == 0), stop=(c == HC - 1))
                        fm.inc(last)
                    FM_G = fm.v - 1
                    FM_GU = fm.v
                    wbf_free[bg] = fm.v
                    wbf_free[bu] = fm.v

                    @block.scalar
                    def _(s, it=it, mt=FM_G):
                        s.wait_ge(S_FFM, mt)
                        fs.inc(s.activation(out=sact[:, (it % 2) * 512:(it % 2 + 1) * 512],
                                            in_=ps_gu[it % 2][:], func=AF.Silu))
                    slot_guard_s[it % 2] = fs.v

                    @block.vector
                    def _(v, it=it, st_=fs.v, mt=FM_GU):
                        v.wait_ge(S_FFS, st_)
                        v.wait_ge(S_FFM, mt)
                        fv.inc(v.tensor_tensor(
                            out=act_cmp[:, it * CAP:(it + 1) * CAP],
                            in0=sact[:, (it % 2) * 512:(it % 2 + 1) * 512],
                            in1=ps_gu[2 + it % 2][:], op=ALU.mult))
                    slot_guard_v[it % 2] = fv.v
                FV_ACT = fv.v
                xs_ready[e] = fm.v
                if e < E - 1:
                    emit_prep(e + 1)

                for ht in range(HC):
                    gd, wtd = emit_wload(rtd_h[e][ht], 2)
                    bd, (cts, ctv) = emit_wcast(gd, wtd, 2)

                    @block.tensor
                    def _(t, ht=ht, bd=bd, cts=cts, ctv=ctv, at=FV_ACT,
                          pg=psy_guard[ht % 2]):
                        t.wait_ge(cts, ctv)
                        t.wait_ge(S_FFV, at)
                        if pg[1]:
                            t.wait_ge(pg[0], pg[1])
                        last = None
                        for ic in range(IT):
                            last = t.matmul(
                                out=ps_y[:, (ht % 2) * 512:(ht % 2 + 1) * 512],
                                lhsT=wbf[:, bd * 1024 + ic * P:bd * 1024 + (ic + 1) * P],
                                rhs=act_cmp[:, ic * CAP:(ic + 1) * CAP],
                                start=(ic == 0), stop=(ic == IT - 1))
                        fm.inc(last)
                    wbf_free[bd] = fm.v
                    wbf_free[bd + 1] = fm.v
                    FM_DN = fm.v

                    @block.vector
                    def _(v, ht=ht, mt=FM_DN, ycf=yc_free):
                        v.wait_ge(S_FFM, mt)
                        if ht == 0 and ycf[1]:
                            v.wait_ge(ycf[0], ycf[1])
                        fv.inc(v.tensor_copy(out=yc[:, ht * 512:(ht + 1) * 512],
                                             in_=ps_y[:, (ht % 2) * 512:(ht % 2 + 1) * 512]))
                    psy_guard[ht % 2] = (S_FFV, fv.v)
                FS_DN = fv.v

                for tl in range(4):
                    tau = 4 * e + tl
                    k = ysc_k[0]
                    ysc_k[0] += 1

                    for hg in range(2):
                        tyj, tyh, tyw = ty_half()

                        @block.tensor
                        def _(t, tl=tl, hg=hg, st_=FS_DN, tyh=tyh, tyw=tyw):
                            if tl == 0 and hg == 0:
                                t.wait_ge(S_FFV, st_)
                            t.wait_ge(tyw[0], tyw[1])
                            last = None
                            for hl in range(4):
                                ht = hg * 4 + hl
                                last = t.transpose(
                                    out=tyh[:, hl * P:(hl + 1) * P],
                                    in_=yc[:, ht * 512 + tl * P:ht * 512 + (tl + 1) * P],
                                    identity=ident[:])
                            fm.inc(last)

                        @block.vector
                        def _(v, k=k, hg=hg, mt=fm.v, tyh=tyh):
                            v.wait_ge(S_FFM, mt)
                            if k >= 2 and hg == 0:
                                v.wait_ge(sems["s_r0"], 64 + 16 * (k - 1))
                            fv.inc(v.tensor_copy(
                                out=y_tok[:, (k % 2) * 1024 + hg * 512:
                                          (k % 2) * 1024 + (hg + 1) * 512],
                                in_=tyh))
                        ty_hist[tyj] = (S_FFV, fv.v)
                    FS_YTOK = fv.v

                    @block.gpsimd
                    def _(g, tau=tau, k=k, st_=FS_YTOK):
                        g.wait_ge(S_FFV, st_)
                        ysc.inc(g.indirect_dma_start(
                            out=y_dram[:],
                            out_offset=IOA(ap=idx_g[:, tau:tau + 1], axis=0),
                            in_=y_tok[:, (k % 2) * 1024:(k % 2 + 1) * 1024],
                            in_offset=None))
                yc_free = (S_FFM, fm.v)

        YSC_ALL = ysc.v

        r2_fv = {}
        for tt in range(NT):
            ldc = yl0 if tt % 2 == 0 else yl1

            @block.sync
            def _(sync, tt=tt, ldc=ldc):
                if tt == 0:
                    sync.wait_ge(sems["s_r0"], YSC_ALL)
                if tt >= 2:
                    sync.wait_ge(S_FFV, r2_fv[tt - 2])
                ldc.inc(sync.dma_start(
                    out=yc[:, (tt % 2) * 1024:(tt % 2 + 1) * 1024],
                    in_=y_dram[tt * P:(tt + 1) * P, :]))
            LD_T = ldc.v

            @block.vector
            def _(v, tt=tt, ldt=LD_T):
                v.wait_ge(sems["s_r1"] if tt % 2 == 0 else S_LD, ldt)
                fv.inc(v.tensor_tensor(
                    out=r2[:, tt * H:(tt + 1) * H],
                    in0=x1_tm[:, tt * H:(tt + 1) * H],
                    in1=yc[:, (tt % 2) * 1024:(tt % 2 + 1) * 1024],
                    op=ALU.add))
            r2_fv[tt] = fv.v

            emit_ln(r2, tt, None)

            @block.sync
            def _(sync, tt=tt, vt=fv.v):
                sync.wait_ge(S_FFV, vt)
                st.inc(sync.dma_start(
                    out=out_h[tt * P:(tt + 1) * P, :],
                    in_=r2[:, tt * H:(tt + 1) * H]))

        stackD.close()

        @block.sync
        def _(sync):
            sync.wait_ge(S_ST, st.v)

    return nc


def _prep_inputs(ins):
    x = np.ascontiguousarray(np.asarray(ins["L0_attn_x"])[0])
    xT = np.ascontiguousarray(x.T)
    w1 = np.ascontiguousarray(
        np.concatenate([np.asarray(ins["L0_attn_x_proj_w"]).T,
                        np.asarray(ins["L0_attn_x_proj_b_w"]).T], axis=1))
    dtw_f = np.asarray(ins["L0_attn_dt_proj_w"])
    dtw_b = np.asarray(ins["L0_attn_dt_proj_b_w"])
    alog_f = np.asarray(ins["L0_attn_A_log"])
    alog_b = np.asarray(ins["L0_attn_A_b_log"])
    dtb_f = np.asarray(ins["L0_attn_dt_proj_b"])
    dtb_b = np.asarray(ins["L0_attn_dt_proj_b_b"])
    dv_f = np.asarray(ins["L0_attn_D"])
    dv_b = np.asarray(ins["L0_attn_D_b"])

    gw = np.ascontiguousarray(np.asarray(ins["L0_ffn_gate_w"]))
    shg = np.ascontiguousarray(np.asarray(ins["L0_ffn_sh_gate"])
                               .reshape(HC, P, IT, P).transpose(2, 1, 0, 3))
    shu = np.ascontiguousarray(np.asarray(ins["L0_ffn_sh_up"])
                               .reshape(HC, P, IT, P).transpose(2, 1, 0, 3))
    shd = np.ascontiguousarray(np.asarray(ins["L0_ffn_sh_down"])
                               .reshape(IT, P, HC, P).transpose(2, 1, 0, 3))
    rtg = np.ascontiguousarray(np.asarray(ins["L0_ffn_rt_gate"])
                               .reshape(E, HC, P, IT, P).transpose(0, 3, 2, 1, 4))
    rtu = np.ascontiguousarray(np.asarray(ins["L0_ffn_rt_up"])
                               .reshape(E, HC, P, IT, P).transpose(0, 3, 2, 1, 4))
    rtd = np.ascontiguousarray(np.asarray(ins["L0_ffn_rt_down"])
                               .reshape(E, IT, P, HC, P).transpose(0, 3, 2, 1, 4))

    in_maps = []
    for k in range(NCORES):
        sl = slice(P * k, P * (k + 1))
        tok = slice(TC * k, TC * (k + 1))
        in_maps.append({
            "xk_tok": np.ascontiguousarray(xT[:, tok]),
            "xk_str": np.ascontiguousarray(xT[sl, :]),
            "w1": w1,
            "dtw": np.ascontiguousarray(
                np.concatenate([dtw_f[sl].T, dtw_b[sl].T], axis=1)),
            "dtb": np.ascontiguousarray(np.stack([dtb_f[sl], dtb_b[sl]], axis=1)),
            "alog": np.ascontiguousarray(np.concatenate([alog_f[sl], alog_b[sl]], axis=1)),
            "dvec": np.ascontiguousarray(np.stack([dv_f[sl], dv_b[sl]], axis=1)),
            "gw": gw, "shg": shg, "shu": shu, "shd": shd,
            "rtg": rtg, "rtu": rtu, "rtd": rtd,
        })
    return in_maps


def kernel(**inputs):
    in_maps = _prep_inputs(inputs)
    nc = build_program(stage=2)
    res = run_bass_kernel_spmd(nc, in_maps, core_ids=list(range(NCORES)))
    out = np.concatenate([res.results[k]["out"] for k in range(NCORES)], axis=0)
    return out.reshape(1, T, H)
